# version 01
# baseline (speedup 1.0000x reference)
import numpy as np
from contextlib import ExitStack
import concourse.bass as bass
import concourse.mybir as mybir
from concourse.bass import IndirectOffsetOnAxis
from concourse.bass_utils import run_bass_kernel_spmd

F32 = mybir.dt.float32
BF16 = mybir.dt.bfloat16
I32 = mybir.dt.int32
U32 = mybir.dt.uint32
I8 = mybir.dt.int8
AF = mybir.ActivationFunctionType
ALU = mybir.AluOpType
AX = mybir.AxisListType

NCORE = 8
T = 2048
D = 1024
NT = T // 128
ND = D // 128

_NC_CACHE = {}


class Sem:
    def __init__(self, handle):
        self.h = handle
        self.n = 0

    def inc(self, k):
        self.n += k
        return self.n


class Prog:

    ENGINES = ("sync", "scalar", "vector", "tensor", "gpsimd")

    def __init__(self, nc):
        self.nc = nc
        self.ops = {e: [] for e in self.ENGINES}
        self._ctx = ExitStack()

    def sem(self, name):
        return Sem(self._ctx.enter_context(self.nc.semaphore(name)))

    def sbuf(self, name, shape, dtype):
        return self._ctx.enter_context(self.nc.sbuf_tensor(name, shape, dtype))

    def psum(self, name, shape, dtype):
        return self._ctx.enter_context(self.nc.psum_tensor(name, shape, dtype))

    def on(self, eng, fn, inc=None):
        if inc is None:
            self.ops[eng].append(fn)
        else:
            sem, k = inc
            def g(e, fn=fn, sem=sem, k=k):
                fn(e).then_inc(sem.h, k)
            self.ops[eng].append(g)
            sem.inc(k)

    def wait(self, eng, sem, val=None):
        v = sem.n if val is None else val
        if v > 0:
            self.ops[eng].append(lambda e, sem=sem, v=v: e.wait_ge(sem.h, v))

    def dma(self, eng, out, in_, sem):
        self.on(eng, lambda e, out=out, in_=in_: e.dma_start(out=out, in_=in_),
                inc=(sem, 16))

    def emit_and_close(self):
        nc = self.nc
        with nc.Block() as block:
            @block.sync
            def _(e):
                for f in self.ops["sync"]:
                    f(e)

            @block.scalar
            def _(e):
                for f in self.ops["scalar"]:
                    f(e)

            @block.vector
            def _(e):
                for f in self.ops["vector"]:
                    f(e)

            @block.tensor
            def _(e):
                for f in self.ops["tensor"]:
                    f(e)

            @block.gpsimd
            def _(e):
                for f in self.ops["gpsimd"]:
                    f(e)
        self._ctx.close()


def emit_ident(p):
    nc = p.nc
    ident_i = p.sbuf("ident_i", [128, 128], I32)
    ident = p.sbuf("ident", [128, 128], BF16)
    gs = p.sem("ident_gs")
    done = p.sem("ident_done")

    def f1(e):
        e.iota(ident_i[:], pattern=[[1, 128]], base=0, channel_multiplier=-1)
        e.drain()
        return e.tensor_scalar(out=ident_i[:], in0=ident_i[:], scalar1=0,
                               scalar2=None, op0=ALU.is_equal)
    p.on("gpsimd", f1, inc=(gs, 1))
    p.wait("vector", gs)
    p.on("vector", lambda e: e.tensor_copy(out=ident[:], in_=ident_i[:]),
         inc=(done, 1))
    return ident, done


def emit_ln(p, name, x, xh, ready_sem, out_sem, ntiles=NT, d=D, eps=1e-5):
    nt = ntiles
    stats = p.sbuf(f"{name}_st", [128, 5 * nt], F32)
    scratch = p.sbuf(f"{name}_sc", [128, 2 * d], F32)
    ssq = p.sem(f"{name}_ssq")
    sst = p.sem(f"{name}_sst")

    p.wait("scalar", ready_sem)
    for i in range(nt):
        xi = x[:, i * d:(i + 1) * d]
        p.on("scalar", lambda e, xi=xi, i=i: e.activation(
            out=scratch[:, 0:d], in_=xi, func=AF.Square,
            accum_out=stats[:, nt + i:nt + i + 1]))
        def fid(e, xi=xi, i=i):
            r = e.activation(out=scratch[:, d:2 * d], in_=xi, func=AF.Identity,
                             accum_out=stats[:, i:i + 1])
            e.drain()
            return r
        p.on("scalar", fid, inc=(ssq, 1))

    p.wait("vector", ssq)

    def fstats(e):
        e.tensor_scalar(out=stats[:, 2 * nt:3 * nt], in0=stats[:, 0:nt],
                        scalar1=1.0 / d, scalar2=None, op0=ALU.mult)
        e.drain()
        e.tensor_tensor(out=stats[:, 4 * nt:5 * nt], in0=stats[:, 2 * nt:3 * nt],
                        in1=stats[:, 2 * nt:3 * nt], op=ALU.mult)
        e.drain()
        e.scalar_tensor_tensor(out=stats[:, 3 * nt:4 * nt],
                               in0=stats[:, nt:2 * nt], scalar=1.0 / d,
                               in1=stats[:, 4 * nt:5 * nt],
                               op0=ALU.mult, op1=ALU.subtract)
        e.drain()
        return e.tensor_scalar(
            out=stats[:, 3 * nt:4 * nt], in0=stats[:, 3 * nt:4 * nt],
            scalar1=eps, scalar2=None, op0=ALU.add)
    p.on("vector", fstats, inc=(sst, 1))
    p.wait("scalar", sst)
    p.on("scalar", lambda e: e.activation(out=stats[:, 3 * nt:4 * nt],
                                          in_=stats[:, 3 * nt:4 * nt],
                                          func=AF.Sqrt), inc=(sst, 1))
    p.wait("vector", sst, 2)

    def frecip(e):
        e.reciprocal(out=stats[:, 3 * nt:4 * nt], in_=stats[:, 3 * nt:4 * nt])
        return e.drain()
    p.on("vector", frecip)
    for i in range(nt):
        def f(e, i=i):
            return e.tensor_scalar(
                out=xh[:, i * d:(i + 1) * d], in0=x[:, i * d:(i + 1) * d],
                scalar1=stats[:, 2 * nt + i:2 * nt + i + 1],
                scalar2=stats[:, 3 * nt + i:3 * nt + i + 1],
                op0=ALU.subtract, op1=ALU.mult)
        p.on("vector", f, inc=(out_sem, 1))
    return stats


def bank_bf16(bank):
    return bank[:].bitcast(BF16)


def emit_tm_to_fm(p, name, xh, fm, ident, ident_sem, in_sem, pbanks,
                  ntiles=NT, nd=ND, tcols=T, in_sem_per_tile=1, tile_off=0,
                  src_tile_off=0):
    pe_s = p.sem(f"{name}_pe")
    ev_s = p.sem(f"{name}_ev")
    d = nd * 128
    p.wait("tensor", ident_sem)
    for i in range(ntiles):
        pt = bank_bf16(pbanks[i % 2]).rearrange("p (c r) -> p c r", r=128)
        si = (src_tile_off + i) * d
        if i >= 2:
            p.wait("tensor", ev_s, i - 1)
        if in_sem is not None:
            p.wait("tensor", in_sem, (i + 1) * in_sem_per_tile)
        for c in range(nd):
            def f(e, pt=pt, si=si, c=c):
                return e.transpose(pt[:, c, :],
                                   in_=xh[:, si + c * 128:si + (c + 1) * 128],
                                   identity=ident[:])
            if c == nd - 1:
                p.on("tensor", f, inc=(pe_s, 1))
            else:
                p.on("tensor", f)
        p.wait("vector", pe_s, i + 1)
        off = (tile_off + i) * 128
        dst = fm[:].rearrange("p (c r) -> p c r", r=tcols)[:, 0:nd, off:off + 128]
        p.on("vector", lambda e, dst=dst, pt=pt: e.tensor_copy(out=dst, in_=pt[:, 0:nd, :]),
             inc=(ev_s, 1))
    return ev_s

def psum_scoped(p, name, shape, dtype):
    g = p.nc.psum_tensor(name, shape, dtype)
    t = g.__enter__()
    return t, g


def build_l0_ffn():
    nc = bass.Bass("TRN2", target_bir_lowering=False, debug=False)
    x_d = nc.dram_tensor("x", [T, D], F32, kind="ExternalInput").ap()
    bdR_d = nc.dram_tensor("bdR", [128, 32 * 128], F32, kind="ExternalInput").ap()
    bdU_d = nc.dram_tensor("bdU", [128, 128 * 128], F32, kind="ExternalInput").ap()
    wcT_d = nc.dram_tensor("wcT", [128, 128 * 1024], F32, kind="ExternalInput").ap()
    y_d = nc.dram_tensor("y", [T, D], F32, kind="ExternalOutput").ap()
    p = Prog(nc)

    x = p.sbuf("xs", [128, NT * D], F32)
    xh = p.sbuf("xh", [128, NT * D], BF16)
    fm = p.sbuf("fm", [128, ND * T], BF16)
    bdR = p.sbuf("bdRs", [128, 32 * 128], BF16)
    bdU = p.sbuf("bdUs", [128, 128 * 128], BF16)
    wc = [p.sbuf(f"wc{j}", [128, 4 * 1024], BF16) for j in range(2)]
    ybuf = [p.sbuf(f"yb{j}", [128, 256], BF16) for j in range(2)]
    hbuf = [p.sbuf(f"hb{j}", [128, 256], BF16) for j in range(2)]

    ld = p.sem("ld")
    wl = [p.sem(f"wl{j}") for j in range(2)]
    wlc = p.sem("wlc")
    lns = p.sem("lns")
    pe_y = p.sem("pe_y")
    act_y = p.sem("act_y")
    pe_z = p.sem("pe_z")
    act_h = p.sem("act_h")
    pe_hc = p.sem("pe_hc")
    dve_c = p.sem("dve_c")
    st = p.sem("st")

    xv = x_d.rearrange("(i p) d -> p i d", p=128)
    for q in range(4):
        p.dma("sync", x[:].rearrange("p (i d) -> p i d", d=D)[:, q * 4:(q + 1) * 4, :],
              xv[:, q * 4:(q + 1) * 4, :], ld)
    p.dma("gpsimd", bdR[:], bdR_d, wlc)
    p.dma("gpsimd", bdU[:], bdU_d, wlc)

    banks = [p.psum(f"bank{j}", [128, 512], F32) for j in range(8)]

    ident, id_done = emit_ident(p)
    emit_ln(p, "ln", x, xh, ld, lns)
    ev_s = emit_tm_to_fm(p, "tp", xh, fm, ident, id_done, lns, banks[0:2])

    for pp in range(32):
        if pp >= 2:
            p.wait("gpsimd", dve_c, 4 * 8 * (pp - 1))
        p.dma("gpsimd", wc[pp % 2][:], wcT_d[:, 4 * pp * 1024:(4 * pp + 4) * 1024], wl[pp % 2])

    psy = banks[0:2]
    psz = banks[2:4]
    psc = banks[4:8]
    p.wait("tensor", ev_s, NT)

    n_y = 0
    n_z = 0
    for pp in range(32):
        h = pp // 8
        if pp == 0:
            p.wait("tensor", wlc, 32)
        p.wait("tensor", wl[pp % 2], 16 * (pp // 2 + 1))
        for o in range(8):
            ts0 = o * 256
            for rgi in range(1):
                rg = pp % 8
                iy = n_y % 2
                if n_y >= 2:
                    p.wait("tensor", act_y, n_y - 1)
                def f(e, h=h, rg=rg, ts0=ts0, iy=iy):
                    return e.matmul(psy[iy][:, 0:256],
                                    lhsT=bdR[:, (h * 8 + rg) * 128:(h * 8 + rg + 1) * 128],
                                    rhs=fm[:, rg * T + ts0:rg * T + ts0 + 256],
                                    start=True, stop=True)
                p.on("tensor", f, inc=(pe_y, 1))
                n_y += 1
                p.wait("scalar", pe_y, n_y)
                p.on("scalar", lambda e, iy=iy: e.copy(
                    out=ybuf[iy][:], in_=psy[iy][:, 0:256]),
                    inc=(act_y, 1))
                for rpl in range(4):
                    c = h * 32 + rg * 4 + rpl
                    hcl = c - 4 * pp
                    iz = n_z % 2
                    if n_z >= 2:
                        p.wait("tensor", act_h, n_z - 1)
                    p.wait("tensor", act_y, n_y)
                    def f(e, c=c, iz=iz, iy=iy):
                        return e.matmul(psz[iz][:, 0:256],
                                        lhsT=bdU[:, c * 128:(c + 1) * 128],
                                        rhs=ybuf[iy][:], start=True, stop=True)
                    p.on("tensor", f, inc=(pe_z, 1))
                    n_z += 1
                    p.wait("scalar", pe_z, n_z)
                    if n_z > 2:
                        p.wait("scalar", pe_hc, n_z - 2)
                    p.on("scalar", lambda e, iz=iz, ih=hcl % 2: e.activation(
                        out=hbuf[ih][:], in_=psz[iz][:, 0:256],
                        func=AF.Gelu), inc=(act_h, 1))
                    p.wait("tensor", act_h, n_z)
                    if hcl == 0 and pp > 0:
                        p.wait("tensor", dve_c, 4 * ((pp - 1) * 8 + o + 1))
                    for tsub in range(2):
                        for oh in range(2):
                            def f(e, hcl=hcl, tsub=tsub, oh=oh, pp=pp, ih=hcl % 2, ts0=ts0):
                                return e.matmul(
                                    psc[tsub * 2 + oh][:],
                                    lhsT=hbuf[ih][:, tsub * 128:tsub * 128 + 128],
                                    rhs=wc[pp % 2][:, hcl * 1024 + oh * 512:hcl * 1024 + oh * 512 + 512],
                                    start=(hcl == 0), stop=(hcl == 3))
                            if tsub == 1 and oh == 1:
                                p.on("tensor", f, inc=(pe_hc, 1))
                            else:
                                p.on("tensor", f)
            p.wait("vector", pe_hc, 4 * (pp * 8 + o + 1))
            for tsub in range(2):
                for oh in range(2):
                    ti = (ts0 // 128) + tsub
                    def f(e, ti=ti, oh=oh, j=tsub * 2 + oh):
                        return e.scalar_tensor_tensor(
                            out=x[:, ti * D + oh * 512:ti * D + oh * 512 + 512],
                            in0=psc[j][:], scalar=0.5,
                            in1=x[:, ti * D + oh * 512:ti * D + oh * 512 + 512],
                            op0=ALU.mult, op1=ALU.add)
                    p.on("vector", f, inc=(dve_c, 1))

    p.wait("sync", dve_c)
    yv = y_d.rearrange("(i p) d -> p i d", p=128)
    for q in range(4):
        p.dma("sync", yv[:, q * 4:(q + 1) * 4, :],
              x[:].rearrange("p (i d) -> p i d", d=D)[:, q * 4:(q + 1) * 4, :], st)
    p.wait("sync", st)
    p.emit_and_close()
    return nc


def _pack_l0_ffn_weights(rail_maker, rail_up_proj, w_cproj):
    rm = np.asarray(rail_maker, np.float32)
    up = np.asarray(rail_up_proj, np.float32)
    wc = np.asarray(w_cproj, np.float32)
    bdR = np.zeros((4, 8, 128, 128), np.float32)
    rmr = rm.reshape(4, 8, 8, 16, 16)
    for rr in range(8):
        bdR[:, :, 16 * rr:16 * rr + 16, 16 * rr:16 * rr + 16] = rmr[:, :, rr]
    bdR_host = np.ascontiguousarray(bdR.transpose(2, 0, 1, 3).reshape(128, 32 * 128))
    bdU = np.zeros((4, 32, 128, 128), np.float32)
    for h in range(4):
        for rpair in range(32):
            for rl in range(2):
                r = 2 * rpair + rl
                rr = r - 8 * (rpair // 4)
                bdU[h, rpair, 16 * rr:16 * rr + 16, 64 * rl:64 * rl + 64] = up[h, r].T
    bdU_host = np.ascontiguousarray(bdU.transpose(2, 0, 1, 3).reshape(128, 128 * 128))
    wcT_host = np.ascontiguousarray(
        wc.T.reshape(128, 128, 1024).transpose(1, 0, 2).reshape(128, 128 * 1024))
    return bdR_host, bdU_host, wcT_host


def run_l0_ffn(x_full, rail_maker, rail_up_proj, w_cproj):
    if "l0_ffn" not in _NC_CACHE:
        _NC_CACHE["l0_ffn"] = build_l0_ffn()
    nc = _NC_CACHE["l0_ffn"]
    bdR_host, bdU_host, wcT_host = _pack_l0_ffn_weights(rail_maker, rail_up_proj, w_cproj)
    in_maps = []
    for c in range(NCORE):
        in_maps.append({
            "x": np.ascontiguousarray(x_full[c * T:(c + 1) * T]),
            "bdR": bdR_host, "bdU": bdU_host, "wcT": wcT_host,
        })
    res = run_bass_kernel_spmd(nc, in_maps, core_ids=list(range(NCORE)))
    return np.concatenate([res.results[c]["y"] for c in range(NCORE)], axis=0)


def emit_ln_tp_stream(p, name, x_view, fm, ident, ident_sem, pbanks,
                      ntiles=NT, d=D, tcols=T, eps=1e-5):
    nd = d // 128
    nt = ntiles
    xw = p.sbuf(f"{name}_xw", [128, 2 * d], F32)
    xhw = p.sbuf(f"{name}_xhw", [128, 2 * d], BF16)
    stats = p.sbuf(f"{name}_st", [128, 5 * nt], F32)
    scratch = p.sbuf(f"{name}_sc", [128, 2 * d], F32)
    lda = [p.sem(f"{name}_lda{j}") for j in range(2)]
    ldb = [p.sem(f"{name}_ldb{j}") for j in range(2)]
    ssq = p.sem(f"{name}_ssq")
    sst = p.sem(f"{name}_sst")
    lns = p.sem(f"{name}_lns")
    pe_s = p.sem(f"{name}_pe")
    ev_s = p.sem(f"{name}_ev")

    for i in range(nt):
        sl = (i % 2) * d
        if i >= 2:
            p.wait("sync", ssq, i - 1)
        p.dma("sync", xw[:, sl:sl + d], x_view[:, i, :], lda[i % 2])
        p.wait("scalar", lda[i % 2], 16 * (i // 2 + 1))
        p.on("scalar", lambda e, sl=sl, i=i: e.activation(
            out=scratch[:, 0:d], in_=xw[:, sl:sl + d], func=AF.Square,
            accum_out=stats[:, nt + i:nt + i + 1]))
        def fid(e, sl=sl, i=i):
            r = e.activation(out=scratch[:, d:2 * d], in_=xw[:, sl:sl + d],
                             func=AF.Identity, accum_out=stats[:, i:i + 1])
            e.drain()
            return r
        p.on("scalar", fid, inc=(ssq, 1))
    p.wait("vector", ssq)

    def fstats(e):
        e.tensor_scalar(out=stats[:, 2 * nt:3 * nt], in0=stats[:, 0:nt],
                        scalar1=1.0 / d, scalar2=None, op0=ALU.mult)
        e.drain()
        e.tensor_tensor(out=stats[:, 4 * nt:5 * nt], in0=stats[:, 2 * nt:3 * nt],
                        in1=stats[:, 2 * nt:3 * nt], op=ALU.mult)
        e.drain()
        e.scalar_tensor_tensor(out=stats[:, 3 * nt:4 * nt],
                               in0=stats[:, nt:2 * nt], scalar=1.0 / d,
                               in1=stats[:, 4 * nt:5 * nt],
                               op0=ALU.mult, op1=ALU.subtract)
        e.drain()
        return e.tensor_scalar(out=stats[:, 3 * nt:4 * nt],
                               in0=stats[:, 3 * nt:4 * nt],
                               scalar1=eps, scalar2=None, op0=ALU.add)
    p.on("vector", fstats, inc=(sst, 1))
    p.wait("scalar", sst)
    p.on("scalar", lambda e: e.activation(out=stats[:, 3 * nt:4 * nt],
                                          in_=stats[:, 3 * nt:4 * nt],
                                          func=AF.Sqrt), inc=(sst, 1))
    p.wait("vector", sst, 2)

    def frecip(e):
        e.reciprocal(out=stats[:, 3 * nt:4 * nt], in_=stats[:, 3 * nt:4 * nt])
        return e.drain()
    p.on("vector", frecip)

    p.wait("tensor", ident_sem)
    p.wait("sync", ssq, nt)
    for i in range(nt):
        sl = (i % 2) * d
        if i >= 2:
            p.wait("sync", lns, i - 1)
        p.dma("sync", xw[:, sl:sl + d], x_view[:, i, :], ldb[i % 2])
        p.wait("vector", ldb[i % 2], 16 * (i // 2 + 1))
        if i >= 2:
            p.wait("vector", pe_s, i - 1)
        def f(e, sl=sl, i=i):
            return e.tensor_scalar(
                out=xhw[:, sl:sl + d], in0=xw[:, sl:sl + d],
                scalar1=stats[:, 2 * nt + i:2 * nt + i + 1],
                scalar2=stats[:, 3 * nt + i:3 * nt + i + 1],
                op0=ALU.subtract, op1=ALU.mult)
        p.on("vector", f, inc=(lns, 1))
        pt = bank_bf16(pbanks[i % 2]).rearrange("p (c r) -> p c r", r=128)
        if i >= 2:
            p.wait("tensor", ev_s, i - 1)
        p.wait("tensor", lns, i + 1)
        for c in range(nd):
            def f2(e, pt=pt, sl=sl, c=c):
                return e.transpose(pt[:, c, :],
                                   in_=xhw[:, sl + c * 128:sl + (c + 1) * 128],
                                   identity=ident[:])
            if c == nd - 1:
                p.on("tensor", f2, inc=(pe_s, 1))
            else:
                p.on("tensor", f2)
        p.wait("vector", pe_s, i + 1)
        dst = fm[:].rearrange("p (c r) -> p c r", r=tcols)[:, 0:nd, i * 128:(i + 1) * 128]
        p.on("vector", lambda e, dst=dst, pt=pt: e.tensor_copy(out=dst, in_=pt[:, 0:nd, :]),
             inc=(ev_s, 1))
    return ev_s


L0A_N = 8192
L0A_G = 512
L0A_NP = 16
L0A_PN = L0A_N // L0A_NP


def build_l0_attn():
    nc = bass.Bass("TRN2", target_bir_lowering=False, debug=False)
    x_d = nc.dram_tensor("x", [T, D], F32, kind="ExternalInput").ap()
    kt_d = nc.dram_tensor("kt", [128, 8 * L0A_N], F32, kind="ExternalInput").ap()
    v_d = nc.dram_tensor("v", [128, 64 * D], F32, kind="ExternalInput").ap()
    y_d = nc.dram_tensor("y", [T, D], F32, kind="ExternalOutput").ap()
    p = Prog(nc)

    fm = p.sbuf("fm", [128, ND * T], BF16)
    S = p.sbuf("S", [128, 4 * L0A_N], BF16)
    xr = p.sbuf("xr", [128, 2 * D], F32)
    kt = [p.sbuf(f"kt{j}", [128, 8 * L0A_PN], BF16) for j in range(2)]
    vv = [p.sbuf(f"vv{j}", [128, 4 * D], BF16) for j in range(2)]
    gfm = [p.sbuf(f"gfm{j}", [128, 8 * 128], BF16) for j in range(2)]
    acc = p.sbuf("acc", [128, 4 * D], F32)
    sq = p.sbuf("sq", [128, 512], F32)
    nstat = p.sbuf("nstat", [128, 4 * 16 + 16], F32)

    ld = p.sem("ld")
    wk = [p.sem(f"wk{j}") for j in range(2)]
    wv = [p.sem(f"wv{j}") for j in range(2)]
    lns = p.sem("lns")
    pe_S = p.sem("pe_S")
    act_S = p.sem("act_S")
    dve_S = p.sem("dve_S")
    dve_n = p.sem("dve_n")
    pe_T = p.sem("pe_T")
    nsq = p.sem("nsq")
    dve_T = p.sem("dve_T")
    act_G = p.sem("act_G")
    pe_V = p.sem("pe_V")
    dve_V = p.sem("dve_V")
    st = [p.sem(f"st{j}") for j in range(2)]

    xv = x_d.rearrange("(i p) d -> p i d", p=128)
    banks = [p.psum(f"bank{j}", [128, 512], F32) for j in range(8)]
    ident, id_done = emit_ident(p)
    ev_s = emit_ln_tp_stream(p, "ln", xv, fm, ident, id_done, banks[6:8])

    psS = banks[0:2]
    psT = banks[2:4]
    psV = banks[4:6]

    ktv = kt_d.rearrange("p (c n) -> p c n", n=L0A_N)
    vvv = v_d.rearrange("p (c o) -> p c o", o=D)

    SCALE = float(np.sqrt(L0A_N))
    n_ld = 0
    kl = 0
    vl = 0
    n_S = 0
    n_T = 0
    n_V = 0
    for g in range(4):
        for np_ in range(L0A_NP):
            if kl >= 2:
                p.wait("gpsimd", dve_S, 4 * (kl - 1))
            p.dma("gpsimd", kt[kl % 2][:].rearrange("p (c n) -> p c n", n=L0A_PN),
                  ktv[:, :, np_ * L0A_PN:(np_ + 1) * L0A_PN], wk[kl % 2])
            ib = kl % 2
            kl += 1
            p.wait("tensor", wk[ib], 16 * ((kl - 1) // 2 + 1))
            if g == 0 and np_ == 0:
                p.wait("tensor", ev_s, NT)
            for ti in range(4):
                gt = g * 4 + ti
                ps = psS[n_S % 2]
                if n_S >= 2:
                    p.wait("tensor", dve_S, n_S - 1)
                for c in range(ND):
                    def f(e, ps=ps, c=c, gt=gt, ib=ib):
                        return e.matmul(
                            ps[:],
                            lhsT=fm[:, c * T + gt * 128:c * T + (gt + 1) * 128],
                            rhs=kt[ib][:, c * L0A_PN:(c + 1) * L0A_PN],
                            start=(c == 0), stop=(c == ND - 1))
                    if c == ND - 1:
                        p.on("tensor", f, inc=(pe_S, 1))
                    else:
                        p.on("tensor", f)
                n_S += 1
                p.wait("scalar", pe_S, n_S)
                col = ti * 16 + np_
                def fsq(e, ps=ps, col=col):
                    r = e.activation(out=sq[:], in_=ps[:], func=AF.Square,
                                     accum_out=nstat[:, col:col + 1])
                    e.drain()
                    return r
                p.on("scalar", fsq, inc=(act_S, 1))
                p.wait("vector", act_S, n_S)
                dst = S[:, ti * L0A_N + np_ * L0A_PN:
                        ti * L0A_N + (np_ + 1) * L0A_PN]
                p.on("vector", lambda e, ps=ps, dst=dst: e.tensor_copy(
                    out=dst, in_=ps[:]), inc=(dve_S, 1))
        p.wait("vector", dve_S, n_S)

        def fnorm(e, g=g):
            e.tensor_reduce(out=nstat[:, 64:68],
                            in_=nstat[:, 0:64].rearrange("p (t c) -> p t c", c=16),
                            axis=AX.X, op=ALU.add)
            e.drain()
            return e.tensor_scalar(out=nstat[:, 68:72], in0=nstat[:, 64:68],
                                   scalar1=1e-30, scalar2=None, op0=ALU.add)
        p.on("vector", fnorm, inc=(dve_n, 1))
        p.wait("scalar", dve_n, dve_n.n)
        p.on("scalar", lambda e: e.activation(out=nstat[:, 68:72], in_=nstat[:, 68:72],
                                              func=AF.Sqrt), inc=(nsq, 1))
        p.wait("vector", nsq, nsq.n)

        def frecip(e):
            e.reciprocal(out=nstat[:, 72:76], in_=nstat[:, 68:72])
            e.drain()
            e.tensor_scalar(out=nstat[:, 72:76], in0=nstat[:, 72:76],
                            scalar1=SCALE, scalar2=None, op0=ALU.mult)
            return e.drain()
        p.on("vector", frecip, inc=(dve_n, 1))

        for np_ in range(L0A_NP):
            if vl >= 2:
                p.wait("gpsimd", pe_V, 8 * (vl - 1))
            p.dma("gpsimd", vv[vl % 2][:].rearrange("p (c o) -> p c o", o=D),
                  vvv[:, np_ * 4:(np_ + 1) * 4, :], wv[vl % 2])
            ibv = vl % 2
            vl += 1
            p.wait("scalar", dve_n, dve_n.n)
            for ti in range(4):
                seg = S[:, ti * L0A_N + np_ * L0A_PN:ti * L0A_N + (np_ + 1) * L0A_PN]
                p.on("scalar", lambda e, seg=seg, ti=ti: e.activation(
                    out=seg, in_=seg, func=AF.Gelu,
                    scale=nstat[:, 72 + ti:73 + ti]), inc=(act_G, 1))
            p.wait("tensor", wv[ibv], 16 * ((vl - 1) // 2 + 1))
            for ti in range(4):
                gt = g * 4 + ti
                p.wait("tensor", act_G, act_G.n - 3 + ti)
                pt = bank_bf16(psT[n_T % 2]).rearrange("p (c r) -> p c r", r=128)
                if n_T >= 2:
                    p.wait("tensor", dve_T, n_T - 1)
                for c in range(4):
                    srcx = S[:, ti * L0A_N + np_ * L0A_PN + c * 128:
                             ti * L0A_N + np_ * L0A_PN + (c + 1) * 128]
                    def f(e, pt=pt, c=c, srcx=srcx):
                        return e.transpose(pt[:, c, :], in_=srcx, identity=ident[:])
                    if c == 3:
                        p.on("tensor", f, inc=(pe_T, 1))
                    else:
                        p.on("tensor", f)
                n_T += 1
                p.wait("vector", pe_T, n_T)
                ig = n_T % 2
                if n_T > 2:
                    p.wait("vector", pe_V, 2 * (n_T - 2))
                p.on("vector", lambda e, pt=pt, ig=ig: e.tensor_copy(
                    out=gfm[ig][:, 0:4 * 128], in_=pt[:, 0:4, :]), inc=(dve_T, 1))
                p.wait("tensor", dve_T, n_T)
                p.wait("tensor", dve_V, n_V)
                for oh in range(2):
                    ps = psV[oh]
                    for c in range(4):
                        def f(e, ps=ps, c=c, oh=oh, ig=ig, ibv=ibv):
                            return e.matmul(
                                ps[:], lhsT=gfm[ig][:, c * 128:(c + 1) * 128],
                                rhs=vv[ibv][:, c * D + oh * 512:c * D + oh * 512 + 512],
                                start=(c == 0), stop=(c == 3))
                        if c == 3:
                            p.on("tensor", f, inc=(pe_V, 1))
                        else:
                            p.on("tensor", f)
                    n_V += 1
                    p.wait("vector", pe_V, n_V)
                    dsta = acc[:, ti * D + oh * 512:ti * D + oh * 512 + 512]
                    if np_ == 0:
                        p.on("vector", lambda e, ps=ps, dsta=dsta: e.tensor_copy(
                            out=dsta, in_=ps[:]), inc=(dve_V, 1))
                    else:
                        p.on("vector", lambda e, ps=ps, dsta=dsta: e.tensor_tensor(
                            out=dsta, in0=ps[:], in1=dsta, op=ALU.add), inc=(dve_V, 1))
        p.wait("vector", dve_V, n_V)
        for ti in range(4):
            gt = g * 4 + ti
            sl = (gt % 2) * D
            if gt >= 2:
                p.wait("sync", st[gt % 2], 16 * (gt // 2))
            p.dma("sync", xr[:, sl:sl + D], xv[:, gt, :], ld)
            p.wait("vector", ld, ld.n)
            def f(e, ti=ti, sl=sl):
                return e.tensor_tensor(out=xr[:, sl:sl + D],
                                       in0=acc[:, ti * D:(ti + 1) * D],
                                       in1=xr[:, sl:sl + D], op=ALU.add)
            p.on("vector", f, inc=(dve_V, 1))
            p.wait("sync", dve_V, dve_V.n)
            p.dma("sync", y_d.rearrange("(i p) d -> p i d", p=128)[:, gt, :],
                  xr[:, sl:sl + D], st[gt % 2])
    p.wait("sync", st[0])
    p.wait("sync", st[1])
    p.emit_and_close()
    return nc


def run_l0_attn(x_full, k, v):
    if "l0_attn" not in _NC_CACHE:
        _NC_CACHE["l0_attn"] = build_l0_attn()
    nc = _NC_CACHE["l0_attn"]
    k = np.asarray(k, np.float32)
    v = np.asarray(v, np.float32)
    kt_host = np.ascontiguousarray(
        k.T.reshape(8, 128, L0A_N).transpose(1, 0, 2).reshape(128, 8 * L0A_N))
    v_host = np.ascontiguousarray(
        v.reshape(64, 128, D).transpose(1, 0, 2).reshape(128, 64 * D))
    in_maps = [{"x": np.ascontiguousarray(x_full[c * T:(c + 1) * T]),
                "kt": kt_host, "v": v_host} for c in range(NCORE)]
    res = run_bass_kernel_spmd(nc, in_maps, core_ids=list(range(NCORE)))
    return np.concatenate([res.results[c]["y"] for c in range(NCORE)], axis=0)


def emit_fp4(p, src, dst, scr, W, sa, done):
    nblk = W // 16
    A, B, C, Dt = scr["A"], scr["B"], scr["C"], scr["D"]
    Bi, sgn, st = scr["Bi"], scr["sgn"], scr["st"]

    def fact(e):
        e.activation(out=A, in_=src, func=AF.Abs)
        return e.activation(out=sgn, in_=src, func=AF.Sign)
    p.on("scalar", fact, inc=(sa, 1))
    p.wait("vector", sa)

    def fq(e):
        Av = A.rearrange("p (g x) -> p g x", x=16)
        e.tensor_reduce(out=st[:, 0:nblk], in_=Av, axis=AX.X, op=ALU.max)
        e.drain()
        e.tensor_scalar(out=st[:, 0:nblk], in0=st[:, 0:nblk], scalar1=6e-12,
                        scalar2=1.0 / 6.0, op0=ALU.max, op1=ALU.mult)
        e.drain()
        e.reciprocal(out=st[:, nblk:2 * nblk], in_=st[:, 0:nblk])
        e.drain()
        inv_bc = st[:, nblk:2 * nblk].unsqueeze(2).broadcast_to([128, nblk, 16])
        e.tensor_tensor(out=Av, in0=Av, in1=inv_bc, op=ALU.mult)
        e.drain()
        e.tensor_scalar(out=Bi, in0=A, scalar1=2.0, scalar2=None, op0=ALU.mult)
        e.drain()
        e.tensor_scalar(out=B, in0=Bi, scalar1=1.0, scalar2=None, op0=ALU.mult)
        e.drain()
        e.scalar_tensor_tensor(out=C, in0=A, scalar=2.0, in1=B,
                               op0=ALU.mult, op1=ALU.subtract)
        e.drain()
        e.tensor_scalar(out=C, in0=C, scalar1=0.5, scalar2=None, op0=ALU.is_gt)
        e.drain()
        e.tensor_tensor(out=B, in0=B, in1=C, op=ALU.add)
        e.drain()
        e.tensor_scalar(out=B, in0=B, scalar1=0.5, scalar2=2.0,
                        op0=ALU.mult, op1=ALU.min)
        e.tensor_scalar(out=C, in0=A, scalar1=5.0, scalar2=None, op0=ALU.is_gt)
        e.tensor_scalar(out=Dt, in0=A, scalar1=3.5, scalar2=None, op0=ALU.is_gt)
        e.drain()
        e.scalar_tensor_tensor(out=C, in0=C, scalar=2.0, in1=Dt,
                               op0=ALU.mult, op1=ALU.add)
        e.drain()
        e.tensor_scalar(out=Dt, in0=A, scalar1=2.5, scalar2=None, op0=ALU.is_gt)
        e.drain()
        e.scalar_tensor_tensor(out=C, in0=C, scalar=1.0, in1=Dt,
                               op0=ALU.add, op1=ALU.mult)
        e.drain()
        e.tensor_tensor(out=A, in0=C, in1=B, op=ALU.add)
        e.drain()
        sc_bc = st[:, 0:nblk].unsqueeze(2).broadcast_to([128, nblk, 16])
        e.tensor_tensor(out=Av, in0=Av, in1=sc_bc, op=ALU.mult)
        e.drain()
        return e.tensor_tensor(out=dst, in0=A, in1=sgn, op=ALU.mult)
    p.on("vector", fq, inc=(done, 1))


def build_l3_wq():
    nc = bass.Bass("TRN2", target_bir_lowering=False, debug=False)
    wg_d = nc.dram_tensor("wg", [384, 1024], F32, kind="ExternalInput").ap()
    wu_d = nc.dram_tensor("wu", [384, 1024], F32, kind="ExternalInput").ap()
    wd_d = nc.dram_tensor("wd", [128, 2816], F32, kind="ExternalInput").ap()
    wgq_d = nc.dram_tensor("wgq", [128, 8 * 384], BF16, kind="ExternalOutput").ap()
    wuq_d = nc.dram_tensor("wuq", [128, 8 * 384], BF16, kind="ExternalOutput").ap()
    wdq_d = nc.dram_tensor("wdq", [128, 22 * 128], BF16, kind="ExternalOutput").ap()
    p = Prog(nc)
    wbuf = p.sbuf("wbuf", [128, 2816], F32)
    qbuf = p.sbuf("qbuf", [128, 2816], BF16)
    scr = {
        "A": p.sbuf("scrA", [128, 2816], F32)[:],
        "B": p.sbuf("scrB", [128, 2816], F32)[:],
        "C": p.sbuf("scrC", [128, 2816], F32)[:],
        "D": p.sbuf("scrD", [128, 2816], F32)[:],
        "sgn": p.sbuf("scrS", [128, 2816], BF16)[:],
        "st": p.sbuf("scrT", [128, 2 * 176], F32)[:],
    }
    scr["Bi"] = scr["B"].bitcast(I32)
    og = p.sbuf("og", [128, 8 * 384], BF16)
    ou = p.sbuf("ou", [128, 8 * 384], BF16)
    od = p.sbuf("od", [128, 22 * 128], BF16)
    banks = [p.psum(f"bank{j}", [128, 512], F32) for j in range(2)]
    ld = p.sem("ld")
    sa = p.sem("sa")
    done = p.sem("done")
    pe_t = p.sem("pe_t")
    ev = p.sem("ev")
    st = p.sem("st")
    ident, id_done = emit_ident(p)

    chunks = []
    for lc in range(3):
        chunks.append((wg_d[lc * 128:(lc + 1) * 128, :], 1024, og, lambda c, lc=lc: c * 384 + lc * 128))
    for lc in range(3):
        chunks.append((wu_d[lc * 128:(lc + 1) * 128, :], 1024, ou, lambda c, lc=lc: c * 384 + lc * 128))
    chunks.append((wd_d, 2816, od, lambda c: c * 128))

    p.wait("tensor", id_done)
    n_t = 0
    for k, (src_ap, W, img, colf) in enumerate(chunks):
        if k > 0:
            p.wait("sync", sa, k)
        p.dma("sync", wbuf[:, 0:W], src_ap, ld)
        p.wait("scalar", ld, 16 * (k + 1))
        if k > 0:
            p.wait("vector", pe_t, n_t)
        s = {kk: (vv if kk == "st" else vv[:, 0:W]) for kk, vv in scr.items()}
        p.wait("scalar", done, k)
        emit_fp4(p, wbuf[:, 0:W], qbuf[:, 0:W], s, W, sa, done)
        p.wait("tensor", done, k + 1)
        nb = W // 128
        for c in range(nb):
            pt = bank_bf16(banks[n_t % 2]).rearrange("p (a r) -> p a r", r=128)
            if n_t >= 2:
                p.wait("tensor", ev, n_t - 1)
            def f(e, pt=pt, c=c):
                return e.transpose(pt[:, 0, :], in_=qbuf[:, c * 128:(c + 1) * 128],
                                   identity=ident[:])
            p.on("tensor", f, inc=(pe_t, 1))
            n_t += 1
            p.wait("vector", pe_t, n_t)
            col = colf(c)
            p.on("vector", lambda e, pt=pt, img=img, col=col: e.tensor_copy(
                out=img[:, col:col + 128], in_=pt[:, 0, :]), inc=(ev, 1))
    p.wait("sync", ev)
    p.dma("sync", wgq_d, og[:], st)
    p.dma("sync", wuq_d, ou[:], st)
    p.dma("sync", wdq_d, od[:], st)
    p.wait("sync", st)
    p.emit_and_close()
    return nc


def run_l3_wq(w_gate, w_up, w_down):
    import ml_dtypes
    if "l3_wq" not in _NC_CACHE:
        _NC_CACHE["l3_wq"] = build_l3_wq()
    nc = _NC_CACHE["l3_wq"]
    wg = np.zeros((3072, 1024), np.float32)
    wg[:2816] = np.asarray(w_gate, np.float32)
    wu = np.zeros((3072, 1024), np.float32)
    wu[:2816] = np.asarray(w_up, np.float32)
    wd = np.asarray(w_down, np.float32)
    in_maps = []
    for c in range(NCORE):
        in_maps.append({
            "wg": np.ascontiguousarray(wg[c * 384:(c + 1) * 384]),
            "wu": np.ascontiguousarray(wu[c * 384:(c + 1) * 384]),
            "wd": np.ascontiguousarray(wd[c * 128:(c + 1) * 128]),
        })
    res = run_bass_kernel_spmd(nc, in_maps, core_ids=list(range(NCORE)))
    bf = ml_dtypes.bfloat16
    wgT = np.zeros((128, 8 * 2816), bf)
    wuT = np.zeros((128, 8 * 2816), bf)
    wdT = np.zeros((128, 22 * 1024), bf)
    for core in range(NCORE):
        rg = res.results[core]["wgq"].reshape(128, 8, 384)
        ru = res.results[core]["wuq"].reshape(128, 8, 384)
        rd = res.results[core]["wdq"].reshape(128, 22, 128)
        for c in range(8):
            lo = c * 2816 + core * 384
            n = min(384, 2816 - core * 384)
            if n > 0:
                wgT[:, lo:lo + n] = rg[:, c, :n]
                wuT[:, lo:lo + n] = ru[:, c, :n]
        for kc in range(22):
            wdT[:, kc * 1024 + core * 128: kc * 1024 + (core + 1) * 128] = rd[:, kc, :]
    return wgT, wuT, wdT


L3_I = 2816
L3_NK = 22


def build_l3_ffn():
    nc = bass.Bass("TRN2", target_bir_lowering=False, debug=False)
    x_d = nc.dram_tensor("x", [T, D], F32, kind="ExternalInput").ap()
    wg_d = nc.dram_tensor("wgq", [128, 8 * L3_I], BF16, kind="ExternalInput").ap()
    wu_d = nc.dram_tensor("wuq", [128, 8 * L3_I], BF16, kind="ExternalInput").ap()
    wd_d = nc.dram_tensor("wdq", [128, L3_NK * 1024], BF16, kind="ExternalInput").ap()
    y_d = nc.dram_tensor("y", [T, D], F32, kind="ExternalOutput").ap()
    p = Prog(nc)

    fm = p.sbuf("fm", [128, ND * T], BF16)
    wg = p.sbuf("wg", [128, 8 * L3_I], BF16)
    wu = p.sbuf("wu", [128, 8 * L3_I], BF16)
    wd = [p.sbuf(f"wd{j}", [128, 2 * 1024], BF16) for j in range(2)]
    hb = p.sbuf("hb", [128, L3_NK * 512], BF16)
    htm = p.sbuf("htm", [128, L3_I], BF16)
    hqt = p.sbuf("hqt", [128, L3_I], BF16)
    gb = p.sbuf("gb", [128, 512], BF16)
    xr = p.sbuf("xr", [128, 2 * D], F32)
    QW = 704
    scr = {
        "A": p.sbuf("scrA", [128, QW], F32)[:],
        "B": p.sbuf("scrB", [128, QW], F32)[:],
        "C": p.sbuf("scrC", [128, QW], F32)[:],
        "D": p.sbuf("scrD", [128, QW], F32)[:],
        "sgn": p.sbuf("scrS", [128, QW], BF16)[:],
        "st": p.sbuf("scrT", [128, 2 * (QW // 16)], F32)[:],
    }
    scr["Bi"] = scr["B"].bitcast(I32)

    ld = p.sem("ld")
    wl = p.sem("wl")
    wdl = [p.sem(f"wdl{j}") for j in range(2)]
    pe_g = p.sem("pe_g")
    pe_u = p.sem("pe_u")
    act_g = p.sem("act_g")
    dve_h = p.sem("dve_h")
    pe_f = p.sem("pe_f")
    ev_f = p.sem("ev_f")
    sa_q = p.sem("sa_q")
    dn_q = p.sem("dn_q")
    pe_b = p.sem("pe_b")
    ev_b = p.sem("ev_b")
    pe_d = p.sem("pe_d")
    dve_o = p.sem("dve_o")
    st = [p.sem(f"st{j}") for j in range(2)]

    banks = [p.psum(f"bank{j}", [128, 512], F32) for j in range(8)]
    ident, id_done = emit_ident(p)
    xv = x_d.rearrange("(i p) d -> p i d", p=128)

    p.dma("sync", wg[:], wg_d, wl)
    p.dma("sync", wu[:], wu_d, wl)

    nt, d = NT, D
    xw = p.sbuf("ln_xw", [128, 2 * d], F32)
    xhw = p.sbuf("ln_xhw", [128, 2 * d], BF16)
    xqw = p.sbuf("ln_xqw", [128, 2 * d], BF16)
    stats = p.sbuf("ln_st", [128, 5 * nt], F32)
    scratch = p.sbuf("ln_sc", [128, 2 * d], F32)
    lda = [p.sem(f"ln_lda{j}") for j in range(2)]
    ldb = [p.sem(f"ln_ldb{j}") for j in range(2)]
    ssq = p.sem("ln_ssq")
    sst = p.sem("ln_sst")
    lns = p.sem("ln_lns")
    pe_s = p.sem("ln_pe")
    ev_s = p.sem("ln_ev")

    for i in range(nt):
        sl = (i % 2) * d
        if i >= 2:
            p.wait("sync", ssq, i - 1)
        p.dma("sync", xw[:, sl:sl + d], xv[:, i, :], lda[i % 2])
        p.wait("scalar", lda[i % 2], 16 * (i // 2 + 1))
        p.on("scalar", lambda e, sl=sl, i=i: e.activation(
            out=scratch[:, 0:d], in_=xw[:, sl:sl + d], func=AF.Square,
            accum_out=stats[:, nt + i:nt + i + 1]))
        def fid(e, sl=sl, i=i):
            r = e.activation(out=scratch[:, d:2 * d], in_=xw[:, sl:sl + d],
                             func=AF.Identity, accum_out=stats[:, i:i + 1])
            e.drain()
            return r
        p.on("scalar", fid, inc=(ssq, 1))
    p.wait("vector", ssq)

    def fstats(e):
        e.tensor_scalar(out=stats[:, 2 * nt:3 * nt], in0=stats[:, 0:nt],
                        scalar1=1.0 / d, scalar2=None, op0=ALU.mult)
        e.drain()
        e.tensor_tensor(out=stats[:, 4 * nt:5 * nt], in0=stats[:, 2 * nt:3 * nt],
                        in1=stats[:, 2 * nt:3 * nt], op=ALU.mult)
        e.drain()
        e.scalar_tensor_tensor(out=stats[:, 3 * nt:4 * nt],
                               in0=stats[:, nt:2 * nt], scalar=1.0 / d,
                               in1=stats[:, 4 * nt:5 * nt],
                               op0=ALU.mult, op1=ALU.subtract)
        e.drain()
        return e.tensor_scalar(out=stats[:, 3 * nt:4 * nt],
                               in0=stats[:, 3 * nt:4 * nt],
                               scalar1=1e-5, scalar2=None, op0=ALU.add)
    p.on("vector", fstats, inc=(sst, 1))
    p.wait("scalar", sst)
    p.on("scalar", lambda e: e.activation(out=stats[:, 3 * nt:4 * nt],
                                          in_=stats[:, 3 * nt:4 * nt],
                                          func=AF.Sqrt), inc=(sst, 1))
    p.wait("vector", sst, 2)

    def frecip(e):
        e.reciprocal(out=stats[:, 3 * nt:4 * nt], in_=stats[:, 3 * nt:4 * nt])
        return e.drain()
    p.on("vector", frecip)

    p.wait("tensor", id_done)
    p.wait("sync", ssq, nt)
    for i in range(nt):
        sl = (i % 2) * d
        if i >= 2:
            p.wait("sync", lns, i - 1)
        p.dma("sync", xw[:, sl:sl + d], xv[:, i, :], ldb[i % 2])
        p.wait("vector", ldb[i % 2], 16 * (i // 2 + 1))
        if i >= 2:
            p.wait("vector", sa_q, 2 * (i - 1))
        def f(e, sl=sl, i=i):
            return e.tensor_scalar(
                out=xhw[:, sl:sl + d], in0=xw[:, sl:sl + d],
                scalar1=stats[:, 2 * nt + i:2 * nt + i + 1],
                scalar2=stats[:, 3 * nt + i:3 * nt + i + 1],
                op0=ALU.subtract, op1=ALU.mult)
        p.on("vector", f, inc=(lns, 1))
        p.wait("scalar", lns, i + 1)
        for half in range(2):
            if i >= 2 or half > 0:
                pass
            s = {kk: (vv if kk == "st" else vv[:, 0:512]) for kk, vv in scr.items()}
            if i >= 2:
                p.wait("tensor", None, None) if False else None
            if half == 0 and i >= 2:
                p.wait("vector", pe_s, i - 1)
            p.wait("scalar", dn_q, 2 * i + half)
            emit_fp4(p, xhw[:, sl + half * 512:sl + (half + 1) * 512],
                     xqw[:, sl + half * 512:sl + (half + 1) * 512], s, 512,
                     sa_q, dn_q)
        pt = bank_bf16(banks[6 + (i % 2)]).rearrange("p (c r) -> p c r", r=128)
        if i >= 2:
            p.wait("tensor", ev_s, i - 1)
        p.wait("tensor", dn_q, 2 * (i + 1))
        for c in range(ND):
            def f2(e, pt=pt, sl=sl, c=c):
                return e.transpose(pt[:, c, :],
                                   in_=xqw[:, sl + c * 128:sl + (c + 1) * 128],
                                   identity=ident[:])
            if c == ND - 1:
                p.on("tensor", f2, inc=(pe_s, 1))
            else:
                p.on("tensor", f2)
        p.wait("vector", pe_s, i + 1)
        dst = fm[:].rearrange("p (c r) -> p c r", r=T)[:, 0:ND, i * 128:(i + 1) * 128]
        p.on("vector", lambda e, dst=dst, pt=pt: e.tensor_copy(out=dst, in_=pt[:, 0:ND, :]),
             inc=(ev_s, 1))

    psG = banks[0:2]
    psU = banks[2:4]
    psT = banks[4:6]
    psD = banks[6:8]
    p.wait("tensor", ev_s, nt)
    p.wait("tensor", wl, 32)
    n_gu = 0
    n_f = 0
    n_b = 0
    n_d = 0
    n_wd = 0
    for tc in range(4):
        ts0 = tc * 512
        if tc > 0:
            p.wait("vector", pe_d, 44 * tc)
        for oc in range(L3_NK):
            ig = n_gu % 2
            if n_gu >= 2:
                p.wait("tensor", dve_h, n_gu - 1)
            for c in range(ND):
                def f(e, c=c, oc=oc, ig=ig, ts0=ts0):
                    return e.matmul(psG[ig][:],
                                    lhsT=wg[:, c * L3_I + oc * 128:c * L3_I + (oc + 1) * 128],
                                    rhs=fm[:, c * T + ts0:c * T + ts0 + 512],
                                    start=(c == 0), stop=(c == ND - 1))
                if c == ND - 1:
                    p.on("tensor", f, inc=(pe_g, 1))
                else:
                    p.on("tensor", f)
            for c in range(ND):
                def f(e, c=c, oc=oc, ig=ig, ts0=ts0):
                    return e.matmul(psU[ig][:],
                                    lhsT=wu[:, c * L3_I + oc * 128:c * L3_I + (oc + 1) * 128],
                                    rhs=fm[:, c * T + ts0:c * T + ts0 + 512],
                                    start=(c == 0), stop=(c == ND - 1))
                if c == ND - 1:
                    p.on("tensor", f, inc=(pe_u, 1))
                else:
                    p.on("tensor", f)
            n_gu += 1
            p.wait("scalar", pe_g, n_gu)
            if n_gu >= 2:
                p.wait("scalar", dve_h, n_gu - 1)
            p.on("scalar", lambda e, ig=ig: e.activation(
                out=gb[:], in_=psG[ig][:], func=AF.Silu), inc=(act_g, 1))
            p.wait("vector", act_g, n_gu)
            p.wait("vector", pe_u, n_gu)
            p.on("vector", lambda e, ig=ig, oc=oc: e.tensor_tensor(
                out=hb[:, oc * 512:(oc + 1) * 512], in0=gb[:], in1=psU[ig][:],
                op=ALU.mult), inc=(dve_h, 1))
        for ts4 in range(4):
            gt = tc * 4 + ts4
            p.wait("tensor", dve_h, n_gu)
            groups = [(0, 8), (8, 16), (16, L3_NK)]
            for (lo, hi) in groups:
                pt = bank_bf16(psT[0]).rearrange("p (c r) -> p c r", r=128)
                p.wait("tensor", ev_f, ev_f.n)
                for kc in range(lo, hi):
                    def f(e, pt=pt, kc=kc, lo=lo, ts4=ts4):
                        return e.transpose(
                            pt[:, kc - lo, :],
                            in_=hb[:, kc * 512 + ts4 * 128:kc * 512 + ts4 * 128 + 128],
                            identity=ident[:])
                    if kc == hi - 1:
                        p.on("tensor", f, inc=(pe_f, 1))
                    else:
                        p.on("tensor", f)
                p.wait("vector", pe_f, pe_f.n)
                dstv = htm[:, lo * 128:hi * 128].rearrange("p (c r) -> p c r", r=128)
                p.on("vector", lambda e, dstv=dstv, pt=pt, n=hi - lo: e.tensor_copy(
                    out=dstv, in_=pt[:, 0:n, :]), inc=(ev_f, 1))
            p.wait("scalar", ev_f, ev_f.n)
            for sli in range(4):
                p.wait("scalar", dn_q, dn_q.n)
                emit_fp4(p, htm[:, sli * QW:(sli + 1) * QW],
                         hqt[:, sli * QW:(sli + 1) * QW], scr, QW, sa_q, dn_q)
            p.wait("tensor", dn_q, dn_q.n)
            for (lo, hi) in groups:
                ptb = bank_bf16(psT[1]).rearrange("p (c r) -> p c r", r=128)
                p.wait("tensor", ev_b, ev_b.n)
                for kc in range(lo, hi):
                    def f(e, ptb=ptb, kc=kc, lo=lo):
                        return e.transpose(ptb[:, kc - lo, :],
                                           in_=hqt[:, kc * 128:(kc + 1) * 128],
                                           identity=ident[:])
                    if kc == hi - 1:
                        p.on("tensor", f, inc=(pe_b, 1))
                    else:
                        p.on("tensor", f)
                p.wait("vector", pe_b, pe_b.n)
                dstv = hb[:].rearrange("p (a r) -> p a r", r=512)[
                    :, lo:hi, ts4 * 128:ts4 * 128 + 128]
                p.on("vector", lambda e, dstv=dstv, ptb=ptb, n=hi - lo: e.tensor_copy(
                    out=dstv, in_=ptb[:, 0:n, :]), inc=(ev_b, 1))
            p.wait("tensor", ev_b, ev_b.n)
            p.wait("tensor", dve_o, dve_o.n)
            for pc in range(11):
                iw = n_wd % 2
                if n_wd >= 2:
                    p.wait("sync", pe_d, n_wd - 1)
                p.dma("sync", wd[iw][:], wd_d[:, pc * 2048:(pc + 1) * 2048], wdl[iw])
                n_wd += 1
                p.wait("tensor", wdl[iw], 16 * ((n_wd - 1) // 2 + 1))
                for kk in range(2):
                    kc = pc * 2 + kk
                    for oh in range(2):
                        def f(e, iw=iw, kk=kk, kc=kc, oh=oh, ts4=ts4):
                            return e.matmul(
                                psD[oh][:],
                                lhsT=hb[:, kc * 512 + ts4 * 128:kc * 512 + ts4 * 128 + 128],
                                rhs=wd[iw][:, kk * 1024 + oh * 512:kk * 1024 + oh * 512 + 512],
                                start=(kc == 0), stop=(kc == L3_NK - 1))
                        if kk == 1 and oh == 1:
                            p.on("tensor", f, inc=(pe_d, 1))
                        else:
                            p.on("tensor", f)
            sl = (gt % 2) * D
            if gt >= 2:
                p.wait("sync", st[gt % 2], 16 * (gt // 2))
            p.dma("sync", xr[:, sl:sl + D], xv[:, gt, :], ld)
            p.wait("vector", ld, ld.n)
            p.wait("vector", pe_d, n_wd)
            for oh in range(2):
                def f(e, oh=oh, sl=sl):
                    return e.scalar_tensor_tensor(
                        out=xr[:, sl + oh * 512:sl + oh * 512 + 512],
                        in0=psD[oh][:], scalar=0.5,
                        in1=xr[:, sl + oh * 512:sl + oh * 512 + 512],
                        op0=ALU.mult, op1=ALU.add)
                p.on("vector", f, inc=(dve_o, 1))
            p.wait("sync", dve_o, dve_o.n)
            p.dma("sync", y_d.rearrange("(i p) d -> p i d", p=128)[:, gt, :],
                  xr[:, sl:sl + D], st[gt % 2])
    p.wait("sync", st[0])
    p.wait("sync", st[1])
    p.emit_and_close()
    return nc


def run_l3_ffn(x_full, wgT, wuT, wdT):
    if "l3_ffn" not in _NC_CACHE:
        _NC_CACHE["l3_ffn"] = build_l3_ffn()
    nc = _NC_CACHE["l3_ffn"]
    in_maps = [{"x": np.ascontiguousarray(x_full[c * T:(c + 1) * T]),
                "wgq": wgT, "wuq": wuT, "wdq": wdT} for c in range(NCORE)]
    res = run_bass_kernel_spmd(nc, in_maps, core_ids=list(range(NCORE)))
    return np.concatenate([res.results[c]["y"] for c in range(NCORE)], axis=0)


def emit_qround(e, src, dst, s_vec, invs_vec, A, B, C, Bi):
    e.tensor_scalar(out=A, in0=src, scalar1=s_vec, scalar2=None, op0=ALU.mult)
    e.drain()
    e.tensor_copy(out=Bi, in_=A)
    e.drain()
    e.tensor_scalar(out=B, in0=Bi, scalar1=1.0, scalar2=None, op0=ALU.mult)
    e.drain()
    e.tensor_tensor(out=C, in0=A, in1=B, op=ALU.subtract)
    e.drain()
    e.tensor_scalar(out=A, in0=C, scalar1=0.5, scalar2=None, op0=ALU.is_gt)
    e.tensor_scalar(out=C, in0=C, scalar1=-0.5, scalar2=None, op0=ALU.is_lt)
    e.drain()
    e.tensor_tensor(out=B, in0=B, in1=A, op=ALU.add)
    e.drain()
    e.tensor_tensor(out=B, in0=B, in1=C, op=ALU.subtract)
    e.drain()
    return e.tensor_scalar(out=dst, in0=B, scalar1=invs_vec, scalar2=None,
                           op0=ALU.mult)


L1F_H = 4096
TCX = 2176


def build_l1_ffn():
    nc = bass.Bass("TRN2", target_bir_lowering=False, debug=False)
    x_d = nc.dram_tensor("x", [T, D], F32, kind="ExternalInput").ap()
    xhalo_d = nc.dram_tensor("xhalo", [128, D], F32, kind="ExternalInput").ap()
    w1T_d = nc.dram_tensor("w1T", [128, 8 * L1F_H], F32, kind="ExternalInput").ap()
    w2T_d = nc.dram_tensor("w2T", [128, 32 * D], F32, kind="ExternalInput").ap()
    cw_d = nc.dram_tensor("cwp", [128, 32 * 3], F32, kind="ExternalInput").ap()
    cb_d = nc.dram_tensor("cbp", [128, 32], F32, kind="ExternalInput").ap()
    y_d = nc.dram_tensor("y", [T, D], F32, kind="ExternalOutput").ap()
    h2_dram = nc.dram_tensor("h2s", [L1F_H, T], BF16, kind="Internal").ap()
    w1q_dram = nc.dram_tensor("w1qs", [128, 8 * L1F_H], BF16, kind="Internal").ap()
    p = Prog(nc)

    w2q = p.sbuf("w2q", [128, 32 * D], BF16)
    wld = [p.sbuf(f"wld{j}", [128, L1F_H], F32) for j in range(2)]
    wqb = p.sbuf("wqb", [128, L1F_H], BF16)
    w1s = [p.sbuf(f"w1s{j}", [128, 8 * 128], BF16) for j in range(2)]
    cw = p.sbuf("cw_s", [128, 32 * 3], F32)
    cb = p.sbuf("cb_s", [128, 32], F32)
    hch = p.sbuf("hch", [128, TCX], BF16)
    cvs = p.sbuf("cvs", [128, T], F32)
    h2c = p.sbuf("h2c", [128, T], BF16)
    sA = p.sbuf("sA", [128, 1024], F32)
    sB = p.sbuf("sB", [128, 1024], F32)
    sC = p.sbuf("sC", [128, 1024], F32)
    sBi = sB[:].bitcast(I32)
    gst = p.sbuf("gst", [128, 8], F32)
    grow = p.sbuf("grow", [1, 520], F32)
    qs = p.sbuf("qs", [128, 4], F32)
    ones_col = p.sbuf("ones_col", [128, 1], BF16)
    ones_row = p.sbuf("ones_row", [1, 128], F32)
    xr = p.sbuf("xr", [128, 2 * D], F32)

    ld = p.sem("ld")
    wl = [p.sem(f"wl{j}") for j in range(2)]
    ab_s = p.sem("ab_s")
    pe_g = p.sem("pe_g")
    dve_g = p.sem("dve_g")
    act_sg = p.sem("act_sg")
    wqd = p.sem("wqd")
    wst = [p.sem(f"wst{j}") for j in range(2)]
    pe_h = p.sem("pe_h")
    act_eh = p.sem("act_eh")
    dve_cv = p.sem("dve_cv")
    act_sl = p.sem("act_sl")
    sth = [p.sem(f"sth{j}") for j in range(2)]
    ldh = [p.sem(f"ldh{j}") for j in range(2)]
    dve_q = p.sem("dve_q")
    pe_t = p.sem("pe_t")
    ev_t = p.sem("ev_t")
    w1l = [p.sem(f"w1l{j}") for j in range(2)]
    pe_o = p.sem("pe_o")
    dve_o = p.sem("dve_o")
    st = [p.sem(f"st{j}") for j in range(2)]

    banks = [p.psum(f"bank{j}", [128, 512], F32) for j in range(8)]
    ident, id_done = emit_ident(p)
    xv = x_d.rearrange("(i p) d -> p i d", p=128)

    p.dma("sync", cw[:], cw_d, ld)
    p.dma("sync", cb[:], cb_d, ld)
    p.on("vector", lambda e: e.memset(ones_col[:], 1.0))
    p.on("vector", lambda e: e.memset(ones_row[:], 1.0), inc=(dve_g, 1))
    p.wait("tensor", dve_g, 1)

    nsum = 0
    for wi, img in enumerate([w1T_d, w2T_d]):
        for j in range(8):
            sl = nsum % 2
            if nsum >= 2:
                p.wait("sync", ab_s, nsum - 1)
            p.dma("sync", wld[sl][:], img[:, j * L1F_H:(j + 1) * L1F_H], wl[sl])
            p.wait("scalar", wl[sl], 16 * (nsum // 2 + 1))
            def fab(e, sl=sl):
                r = e.activation(out=hch[:].bitcast(BF16) if False else wqb[:],
                                 in_=wld[sl][:], func=AF.Abs)
                e.drain()
                return r
            p.on("scalar", fab, inc=(ab_s, 1))
            nsum += 1
            p.wait("tensor", ab_s, nsum)
            bank = banks[wi]
            for c in range(8):
                first = (j == 0 and c == 0)
                last = (j == 7 and c == 7)
                def f(e, c=c, bank=bank, first=first, last=last):
                    return e.matmul(bank[0:1, :], lhsT=ones_col[:],
                                    rhs=wqb[:, c * 512:(c + 1) * 512],
                                    start=first, stop=last)
                if c == 7:
                    p.on("tensor", f, inc=(pe_g, 1))
                else:
                    p.on("tensor", f)
            if nsum < 16:
                p.wait("scalar", pe_g, nsum)
    p.wait("vector", pe_g, 16)

    def fglob(e):
        e.tensor_copy(out=grow[:, 0:512], in_=banks[0][0:1, :])
        e.tensor_copy(out=grow[:, 512:514].rearrange("p a -> p a") if False else grow[:, 512:514], in_=banks[1][0:1, 0:2])
        e.drain()
        e.tensor_copy(out=grow[:, 514:516], in_=banks[1][0:1, 2:4]) if False else None
        e.tensor_reduce(out=grow[:, 516:517], in_=grow[:, 0:512], axis=AX.X, op=ALU.add)
        e.drain()
        return e.nop()
    def fglob2(e):
        e.tensor_reduce(out=grow[:, 512:513], in_=banks[0][0:1, :], axis=AX.X, op=ALU.add)
        e.tensor_reduce(out=grow[:, 513:514], in_=banks[1][0:1, :], axis=AX.X, op=ALU.add)
        e.drain()
        e.tensor_scalar(out=grow[:, 514:516], in0=grow[:, 512:514],
                        scalar1=1.0 / (L1F_H * D), scalar2=1e-5,
                        op0=ALU.mult, op1=ALU.max)
        e.drain()
        e.tensor_scalar(out=grow[:, 516:518], in0=grow[:, 514:516],
                        scalar1=0.5, scalar2=None, op0=ALU.mult)
        e.tensor_copy(out=grow[:, 518:519], in_=grow[:, 514:515])
        e.tensor_scalar(out=grow[:, 519:520], in0=grow[:, 515:516],
                        scalar1=0.5, scalar2=None, op0=ALU.mult)
        return e.drain()
    p.on("vector", fglob2, inc=(dve_g, 1))
    p.wait("tensor", dve_g, dve_g.n)

    def fbc(e):
        return e.matmul(banks[2][:, 0:4], lhsT=ones_row[:], rhs=grow[:, 516:520],
                        start=True, stop=True)
    p.on("tensor", fbc, inc=(pe_g, 1))
    p.wait("vector", pe_g, 17)
    p.on("vector", lambda e: e.tensor_copy(out=gst[:, 0:4], in_=banks[2][:, 0:4]),
         inc=(dve_g, 1))

    nsum2 = 0
    n_w1st = 0
    p.wait("sync", dve_g, dve_g.n)
    for wi, img in enumerate([w1T_d, w2T_d]):
        for j in range(8):
            sl = nsum2 % 2
            if True:
                p.wait("sync", act_sg, max(0, nsum2 - 1))
                p.wait("sync", wqd, max(0, nsum2 - 1))
            p.dma("sync", wld[sl][:], img[:, j * L1F_H:(j + 1) * L1F_H], wl[sl])
            p.wait("scalar", wl[sl], 16 * (8 + nsum2 // 2 + 1))
            p.wait("scalar", wqd, wqd.n)
            p.wait("scalar", wst[0], 16 * n_w1st)
            def fsg(e, sl=sl):
                r = e.activation(out=wqb[:], in_=wld[sl][:], func=AF.Sign)
                return e.drain()
            p.on("scalar", fsg, inc=(act_sg, 1))
            nsum2 += 1
            p.wait("vector", act_sg, nsum2)
            thr_col = wi
            def fq(e, thr_col=thr_col, wi=wi, j=j, sl=sl):
                for q4 in range(4):
                    lo = q4 * 1024
                    e.tensor_tensor(out=sA[:], in0=wld[sl][:, lo:lo + 1024],
                                    in1=wqb[:, lo:lo + 1024], op=ALU.mult)
                    e.drain()
                    e.tensor_scalar(out=sA[:], in0=sA[:], scalar1=gst[:, thr_col:thr_col + 1],
                                    scalar2=None, op0=ALU.is_gt)
                    e.drain()
                    if wi == 0:
                        r = e.tensor_tensor(out=wqb[:, lo:lo + 1024],
                                            in0=wqb[:, lo:lo + 1024], in1=sA[:], op=ALU.mult)
                    else:
                        r = e.tensor_tensor(out=w2q[:, j * L1F_H + lo:j * L1F_H + lo + 1024],
                                            in0=wqb[:, lo:lo + 1024], in1=sA[:], op=ALU.mult)
                    e.drain()
                return r
            p.on("vector", fq, inc=(wqd, 1))
            if wi == 0:
                p.wait("sync", wqd, wqd.n)
                p.dma("sync", w1q_dram[:, j * L1F_H:(j + 1) * L1F_H], wqb[:], wst[0])
                n_w1st += 1

    nt, d = NT, D
    xw = p.sbuf("ln_xw", [128, 2 * d], F32)
    xhw = p.sbuf("ln_xhw", [128, 2 * d], BF16)
    xqw = p.sbuf("ln_xqw", [128, 2 * d], BF16)
    stats = p.sbuf("ln_st", [128, 5 * (nt + 1)], F32)
    scratch = p.sbuf("ln_sc", [128, 2 * d], F32)
    fm_g = nc.sbuf_tensor("fm", [128, ND * TCX], BF16)
    fm = fm_g.__enter__()

    def fmz(e):
        e.memset(fm[:], 0.0)
        return e.drain()
    p.on("vector", fmz)
    lda = [p.sem(f"ln_lda{j}") for j in range(2)]
    ldb = [p.sem(f"ln_ldb{j}") for j in range(2)]
    ssq = p.sem("ln_ssq")
    sst = p.sem("ln_sst")
    lns = p.sem("ln_lns")
    dn_q = p.sem("ln_dnq")
    pe_s = p.sem("ln_pe")
    ev_s = p.sem("ln_ev")
    ntt = nt + 1

    for i in range(ntt):
        sl = (i % 2) * d
        if i >= 2:
            p.wait("sync", ssq, i - 1)
        srcv = xv[:, i, :] if i < nt else xhalo_d
        p.dma("sync", xw[:, sl:sl + d], srcv, lda[i % 2])
        p.wait("scalar", lda[i % 2], 16 * (i // 2 + 1))
        p.on("scalar", lambda e, sl=sl, i=i: e.activation(
            out=scratch[:, 0:d], in_=xw[:, sl:sl + d], func=AF.Square,
            accum_out=stats[:, ntt + i:ntt + i + 1]))
        def fid(e, sl=sl, i=i):
            r = e.activation(out=scratch[:, d:2 * d], in_=xw[:, sl:sl + d],
                             func=AF.Identity, accum_out=stats[:, i:i + 1])
            e.drain()
            return r
        p.on("scalar", fid, inc=(ssq, 1))
    p.wait("vector", ssq)

    def fstats(e):
        e.tensor_scalar(out=stats[:, 2 * ntt:3 * ntt], in0=stats[:, 0:ntt],
                        scalar1=1.0 / d, scalar2=None, op0=ALU.mult)
        e.drain()
        e.tensor_tensor(out=stats[:, 4 * ntt:5 * ntt], in0=stats[:, 2 * ntt:3 * ntt],
                        in1=stats[:, 2 * ntt:3 * ntt], op=ALU.mult)
        e.drain()
        e.scalar_tensor_tensor(out=stats[:, 3 * ntt:4 * ntt],
                               in0=stats[:, ntt:2 * ntt], scalar=1.0 / d,
                               in1=stats[:, 4 * ntt:5 * ntt],
                               op0=ALU.mult, op1=ALU.subtract)
        e.drain()
        return e.tensor_scalar(out=stats[:, 3 * ntt:4 * ntt],
                               in0=stats[:, 3 * ntt:4 * ntt],
                               scalar1=1e-5, scalar2=None, op0=ALU.add)
    p.on("vector", fstats, inc=(sst, 1))
    p.wait("scalar", sst)
    p.on("scalar", lambda e: e.activation(out=stats[:, 3 * ntt:4 * ntt],
                                          in_=stats[:, 3 * ntt:4 * ntt],
                                          func=AF.Sqrt), inc=(sst, 1))
    p.wait("vector", sst, 2)

    def frecip(e):
        e.reciprocal(out=stats[:, 3 * ntt:4 * ntt], in_=stats[:, 3 * ntt:4 * ntt])
        return e.drain()
    p.on("vector", frecip)

    p.wait("tensor", id_done)
    p.wait("sync", ssq, ntt)
    for i in range(ntt):
        sl = (i % 2) * d
        if i >= 2:
            p.wait("sync", dn_q, i - 1)
        srcv = xv[:, i, :] if i < nt else xhalo_d
        p.dma("sync", xw[:, sl:sl + d], srcv, ldb[i % 2])
        p.wait("vector", ldb[i % 2], 16 * (i // 2 + 1))
        if i >= 2:
            p.wait("vector", pe_s, i - 1)

        def fq2(e, sl=sl, i=i):
            e.tensor_scalar(out=xhw[:, sl:sl + d], in0=xw[:, sl:sl + d],
                            scalar1=stats[:, 2 * ntt + i:2 * ntt + i + 1],
                            scalar2=stats[:, 3 * ntt + i:3 * ntt + i + 1],
                            op0=ALU.subtract, op1=ALU.mult)
            e.drain()
            e.tensor_reduce(out=qs[:, 0:1], in_=xhw[:, sl:sl + d], axis=AX.X,
                            op=ALU.max, apply_absolute_value=True)
            e.drain()
            e.tensor_scalar(out=qs[:, 1:2], in0=qs[:, 0:1], scalar1=1e-5,
                            scalar2=1.0 / 127.0, op0=ALU.max, op1=ALU.mult)
            e.drain()
            e.reciprocal(out=qs[:, 2:3], in_=qs[:, 1:2])
            e.drain()
            return emit_qround(e, xhw[:, sl:sl + d], xqw[:, sl:sl + d],
                               qs[:, 2:3], qs[:, 1:2],
                               sA[:], sB[:], sC[:], sBi[:])
        p.on("vector", fq2, inc=(dn_q, 1))
        pt = bank_bf16(banks[6 + (i % 2)]).rearrange("p (c r) -> p c r", r=128)
        if i >= 2:
            p.wait("tensor", ev_s, i - 1)
        p.wait("tensor", dn_q, i + 1)
        for c in range(ND):
            def f2(e, pt=pt, sl=sl, c=c):
                return e.transpose(pt[:, c, :],
                                   in_=xqw[:, sl + c * 128:sl + (c + 1) * 128],
                                   identity=ident[:])
            if c == ND - 1:
                p.on("tensor", f2, inc=(pe_s, 1))
            else:
                p.on("tensor", f2)
        p.wait("vector", pe_s, i + 1)
        if i < nt:
            dst = fm[:].rearrange("p (c r) -> p c r", r=TCX)[:, 0:ND, 1 + i * 128:1 + (i + 1) * 128]
            p.on("vector", lambda e, dst=dst, pt=pt: e.tensor_copy(out=dst, in_=pt[:, 0:ND, :]),
                 inc=(ev_s, 1))
        else:
            fmv = fm[:].rearrange("p (c r) -> p c r", r=TCX)
            def fev(e, pt=pt, fmv=fmv):
                e.tensor_copy(out=fmv[:, 0:ND, 0:1], in_=pt[:, 0:ND, 0:1])
                e.drain()
                return e.tensor_copy(out=fmv[:, 0:ND, 2049:2050], in_=pt[:, 0:ND, 1:2])
            p.on("vector", fev, inc=(ev_s, 1))

    segs = [(0, 512), (512, 1024), (1024, 1536), (1536, 2048), (2048, 2176)]
    w1v = w1q_dram.rearrange("p (c k) -> p c k", k=L1F_H)
    p.wait("tensor", ev_s, ntt)
    p.wait("vector", ld, 32)
    p.wait("scalar", ld, 32)
    n_seg = 0
    for hc in range(32):
        sl = hc % 2
        if hc >= 2:
            p.wait("sync", pe_h, 5 * (hc - 1))
        p.dma("sync", w1s[sl][:].rearrange("p (c k) -> p c k", k=128),
              w1v[:, :, hc * 128:(hc + 1) * 128], w1l[sl])
        p.wait("tensor", w1l[sl], 16 * (hc // 2 + 1))
        for si, (lo, hi) in enumerate(segs):
            bk = banks[n_seg % 2]
            if n_seg >= 2:
                p.wait("tensor", act_eh, n_seg - 1)
            for c in range(ND):
                def f(e, bk=bk, c=c, sl=sl, lo=lo, hi=hi):
                    return e.matmul(bk[:, 0:hi - lo], lhsT=w1s[sl][:, c * 128:(c + 1) * 128],
                                    rhs=fm[:, c * TCX + lo:c * TCX + hi],
                                    start=(c == 0), stop=(c == ND - 1))
                if c == ND - 1:
                    p.on("tensor", f, inc=(pe_h, 1))
                else:
                    p.on("tensor", f)
            n_seg += 1
            p.wait("scalar", pe_h, n_seg)
            if si == 0 and hc > 0:
                p.wait("scalar", dve_cv, hc)
            p.on("scalar", lambda e, bk=bk, lo=lo, hi=hi: e.copy(
                out=hch[:, lo:hi], in_=bk[:, 0:hi - lo]), inc=(act_eh, 1))
        p.wait("vector", act_eh, n_seg)

        def fcv(e, hc=hc):
            e.tensor_scalar(out=cvs[:], in0=hch[:, 0:2048],
                            scalar1=cw[:, hc * 3:hc * 3 + 1], scalar2=None, op0=ALU.mult)
            e.drain()
            e.scalar_tensor_tensor(out=cvs[:], in0=hch[:, 1:2049],
                                   scalar=cw[:, hc * 3 + 1:hc * 3 + 2], in1=cvs[:],
                                   op0=ALU.mult, op1=ALU.add)
            e.drain()
            return e.scalar_tensor_tensor(out=cvs[:], in0=hch[:, 2:2050],
                                          scalar=cw[:, hc * 3 + 2:hc * 3 + 3], in1=cvs[:],
                                          op0=ALU.mult, op1=ALU.add)
        p.on("vector", fcv, inc=(dve_cv, 1))
        p.wait("scalar", dve_cv, hc + 1)
        if hc >= 1:
            p.wait("scalar", sth[0], 16 * hc)
        p.on("scalar", lambda e, hc=hc: e.activation(
            out=h2c[:], in_=cvs[:], func=AF.Silu,
            scale=gst[:, 2:3], bias=cb[:, hc:hc + 1]), inc=(act_sl, 1))
        p.wait("sync", act_sl, hc + 1)
        p.dma("sync", h2_dram[hc * 128:(hc + 1) * 128, :], h2c[:], sth[0])

    fm_g.__exit__(None, None, None)
    htm = p.sbuf("htm", [128, L1F_H], BF16)
    hqf = p.sbuf("hqf", [128, 32 * 128], BF16)
    p.wait("sync", sth[0])
    n_t2 = 0
    for i in range(nt):
        sl = i % 2
        if i >= 1:
            p.wait("sync", ev_t, 4 * i)
        p.on("sync", lambda e, i=i: e.dma_start(
            out=htm[:], in_=h2_dram[:, i * 128:(i + 1) * 128], transpose=True),
            inc=(ldh[0], 16))
        p.wait("vector", ldh[0], ldh[0].n)

        def fq3(e, i=i):
            e.tensor_reduce(out=qs[:, 0:1], in_=htm[:], axis=AX.X,
                            op=ALU.max, apply_absolute_value=True)
            e.drain()
            e.tensor_scalar(out=qs[:, 1:2], in0=qs[:, 0:1], scalar1=1e-5,
                            scalar2=1.0 / 127.0, op0=ALU.max, op1=ALU.mult)
            e.drain()
            e.reciprocal(out=qs[:, 2:3], in_=qs[:, 1:2])
            e.drain()
            for q4 in range(4):
                lo = q4 * 1024
                r = emit_qround(e, htm[:, lo:lo + 1024], htm[:, lo:lo + 1024],
                                qs[:, 2:3], qs[:, 1:2],
                                sA[:], sB[:], sC[:], sBi[:])
                e.drain()
            return r
        p.on("vector", fq3, inc=(dve_q, 1))
        p.wait("tensor", dve_q, i + 1)
        for g4 in range(4):
            pt = bank_bf16(banks[2 + g4 % 2]).rearrange("p (c r) -> p c r", r=128)
            p.wait("tensor", ev_t, max(0, ev_t.n - 1))
            for c in range(8):
                kc = g4 * 8 + c
                def f(e, pt=pt, c=c, kc=kc):
                    return e.transpose(pt[:, c, :], in_=htm[:, kc * 128:(kc + 1) * 128],
                                       identity=ident[:])
                if c == 7:
                    p.on("tensor", f, inc=(pe_t, 1))
                else:
                    p.on("tensor", f)
            p.wait("vector", pe_t, pe_t.n)
            dstv = hqf[:].rearrange("p (c r) -> p c r", r=128)[:, g4 * 8:(g4 + 1) * 8, :]
            p.on("vector", lambda e, dstv=dstv, pt=pt: e.tensor_copy(
                out=dstv, in_=pt[:, 0:8, :]), inc=(ev_t, 1))
        p.wait("tensor", ev_t, ev_t.n)
        p.wait("tensor", dve_o, dve_o.n)
        for kc in range(32):
            for oh in range(2):
                def f(e, kc=kc, oh=oh):
                    return e.matmul(banks[4 + oh][:],
                                    lhsT=hqf[:, kc * 128:(kc + 1) * 128],
                                    rhs=w2q[:, kc * D + oh * 512:kc * D + oh * 512 + 512],
                                    start=(kc == 0), stop=(kc == 31))
                if kc == 31 and oh == 1:
                    p.on("tensor", f, inc=(pe_o, 1))
                else:
                    p.on("tensor", f)
        n_t2 += 1
        slr = (i % 2) * D
        if i >= 2:
            p.wait("sync", st[i % 2], 16 * (i // 2))
        p.dma("sync", xr[:, slr:slr + D], xv[:, i, :], ld)
        p.wait("vector", ld, ld.n)
        p.wait("vector", pe_o, n_t2)
        for oh in range(2):
            def f(e, oh=oh, slr=slr):
                return e.scalar_tensor_tensor(
                    out=xr[:, slr + oh * 512:slr + oh * 512 + 512],
                    in0=banks[4 + oh][:], scalar=gst[:, 3:4],
                    in1=xr[:, slr + oh * 512:slr + oh * 512 + 512],
                    op0=ALU.mult, op1=ALU.add)
            p.on("vector", f, inc=(dve_o, 1))
        p.wait("sync", dve_o, dve_o.n)
        p.dma("sync", y_d.rearrange("(i p) d -> p i d", p=128)[:, i, :],
              xr[:, slr:slr + D], st[i % 2])
    p.wait("sync", st[0])
    p.wait("sync", st[1])
    p.emit_and_close()
    return nc


def run_l1_ffn(x_full, w1, conv_w, conv_b, w2):
    if "l1_ffn" not in _NC_CACHE:
        _NC_CACHE["l1_ffn"] = build_l1_ffn()
    nc = _NC_CACHE["l1_ffn"]
    w1 = np.asarray(w1, np.float32)
    w2 = np.asarray(w2, np.float32)
    cwp = np.ascontiguousarray(np.asarray(conv_w, np.float32).reshape(32, 128, 3)
                               .transpose(1, 0, 2).reshape(128, 32 * 3))
    cbp = np.ascontiguousarray(np.asarray(conv_b, np.float32).reshape(32, 128).T)
    w1T = np.ascontiguousarray(w1.T.reshape(8, 128, 4096).transpose(1, 0, 2).reshape(128, 8 * 4096))
    w2T = np.ascontiguousarray(w2.T.reshape(32, 128, 1024).transpose(1, 0, 2).reshape(128, 32 * 1024))
    in_maps = []
    for c in range(NCORE):
        halo = np.zeros((128, D), np.float32)
        if c % 2 == 1:
            halo[0] = x_full[c * T - 1]
        else:
            halo[1] = x_full[c * T + T]
        in_maps.append({"x": np.ascontiguousarray(x_full[c * T:(c + 1) * T]),
                        "xhalo": halo, "w1T": w1T, "w2T": w2T, "cwp": cwp, "cbp": cbp})
    res = run_bass_kernel_spmd(nc, in_maps, core_ids=list(range(NCORE)))
    return np.concatenate([res.results[c]["y"] for c in range(NCORE)], axis=0)


S4 = 4096
NT4 = 32


def build_l1_attn():
    nc = bass.Bass("TRN2", target_bir_lowering=False, debug=False)
    x_d = nc.dram_tensor("x", [S4, D], F32, kind="ExternalInput").ap()
    w1_d = nc.dram_tensor("w1p", [128, 8 * 3072], F32, kind="ExternalInput").ap()
    b1_d = nc.dram_tensor("b1p", [128, 32], F32, kind="ExternalInput").ap()
    wp_d = nc.dram_tensor("wpp", [128, 8 * D], F32, kind="ExternalInput").ap()
    bp_d = nc.dram_tensor("bpp", [1, D], F32, kind="ExternalInput").ap()
    flagv_d = nc.dram_tensor("flagv", [128, 2], F32, kind="ExternalInput").ap()
    y_d = nc.dram_tensor("y", [T, D], F32, kind="ExternalOutput").ap()
    p = Prog(nc)

    fm = p.sbuf("fm", [128, ND * S4], BF16)
    yfm = p.sbuf("yfm", [128, ND * T], BF16)
    w1h = [p.sbuf(f"w1h{j}", [128, 8 * 192], BF16) for j in range(2)]
    b1s = p.sbuf("b1s", [128, 32], F32)
    wp = p.sbuf("wps", [128, 8 * D], BF16)
    bp = p.sbuf("bps", [1, D], BF16)
    flg = p.sbuf("flg", [128, 2], F32)
    qh = p.sbuf("qh", [128, 2048], BF16)
    kh = p.sbuf("kh", [128, S4], BF16)
    vtm = p.sbuf("vtm", [128, 32 * 64], BF16)
    Sb = p.sbuf("Sb", [128, S4], F32)
    Pb = p.sbuf("Pb", [128, S4], BF16)
    ptb = p.sbuf("ptb", [128, 8 * 128], BF16)
    msk = p.sbuf("msk", [128, 4 * 512], BF16)
    mskb = p.sbuf("mskb", [128, 4 * 512], BF16)
    mio = p.sbuf("mio", [128, 512], I32)
    sst_t = p.sbuf("sst_t", [128, 8], F32)
    ld = p.sem("ld")
    wl = [p.sem(f"wl{j}") for j in range(2)]
    cst = p.sem("cst")
    pe_p = p.sem("pe_p")
    ev_p = p.sem("ev_p")
    pe_vt = p.sem("pe_vt")
    ev_vt = p.sem("ev_vt")
    pe_S = p.sem("pe_S")
    dve_S = p.sem("dve_S")
    act_P = p.sem("act_P")
    dve_P = p.sem("dve_P")
    pe_T = p.sem("pe_T")
    ev_T = p.sem("ev_T")
    pe_V = p.sem("pe_V")
    ev_V = p.sem("ev_V")
    pe_O = p.sem("pe_O")
    dve_O = p.sem("dve_O")
    ldr = p.sem("ldr")
    st = [p.sem(f"st{j}") for j in range(2)]
    xr = p.sbuf("xr", [128, 2 * D], F32)

    banks = [p.psum(f"bank{j}", [128, 512], F32) for j in range(8)]
    ident, id_done = emit_ident(p)
    xv = x_d.rearrange("(i p) d -> p i d", p=128)

    ldg = p.sem("ldg")
    p.dma("sync", b1s[:], b1_d, ld)
    p.dma("sync", flg[:], flagv_d, ld)
    p.on("gpsimd", lambda e: e.dma_start(out=wp[:], in_=wp_d), inc=(ldg, 16))
    p.on("gpsimd", lambda e: e.dma_start(out=bp[:], in_=bp_d), inc=(ldg, 16))

    def fmsk(e):
        e.iota(mio[:], pattern=[[1, 512]], base=0, channel_multiplier=-1)
        e.drain()
        for r in range(4):
            e.tensor_scalar(out=msk[:, r * 512:(r + 1) * 512], in0=mio[:],
                            scalar1=128 * r, scalar2=None, op0=ALU.is_le)
        e.drain()
        for r in range(4):
            e.tensor_scalar(out=mskb[:, r * 512:(r + 1) * 512],
                            in0=msk[:, r * 512:(r + 1) * 512],
                            scalar1=-1.0, scalar2=1.0e6, op0=ALU.add, op1=ALU.mult)
        e.drain()
        return e.engine_nop() if hasattr(e, 'engine_nop') else e.nop()
    p.on("gpsimd", fmsk, inc=(cst, 1))

    ev_s = emit_ln_tp_stream(p, "ln", xv, fm, ident, id_done, banks[6:8],
                             ntiles=NT4, tcols=S4)

    ones_row_bf = p.sbuf("ones_rb", [1, 128], BF16)
    p.on("vector", lambda e: e.memset(ones_row_bf[:], 1.0), inc=(cst, 1))

    SCL = 1.0 / 8.0
    n_p = 0
    n_vt = 0
    n_S = 0
    n_qt = 0
    n_T = 0
    n_V = 0
    n_O = 0
    for h in range(16):
        sl = h % 2
        if h >= 2:
            p.wait("gpsimd", pe_p, (h - 1) * 20)
        p.on("gpsimd", lambda e, sl=sl, h=h: e.dma_start(
            out=w1h[sl][:].rearrange("p (c k) -> p c k", k=192),
            in_=w1_d.rearrange("p (c k) -> p c k", k=3072)[:, :, h * 192:(h + 1) * 192]),
            inc=(wl[sl], 16))
        p.wait("tensor", wl[sl], 16 * (h // 2 + 1))
        if h == 0:
            p.wait("tensor", ev_s, NT4)
            p.wait("tensor", ld, ld.n)
            p.wait("tensor", ldg, 32)
            p.wait("tensor", cst, 2)
        projs = ([("q", 0, tc) for tc in range(4)] +
                 [("k", 64, tc) for tc in range(8)] +
                 [("v", 128, tc) for tc in range(8)])
        for kind, wof, tc in projs:
            bk = banks[n_p % 2]
            if n_p >= 2:
                p.wait("tensor", ev_p, n_p - 1)
            for c in range(ND):
                def f(e, bk=bk, c=c, wof=wof, tc=tc, sl=sl):
                    return e.matmul(bk[0:64, :],
                                    lhsT=w1h[sl][:, c * 192 + wof:c * 192 + wof + 64],
                                    rhs=fm[:, c * S4 + tc * 512:c * S4 + (tc + 1) * 512],
                                    start=(c == 0), stop=(c == ND - 1))
                if c == ND - 1:
                    p.on("tensor", f, inc=(pe_p, 1))
                else:
                    p.on("tensor", f)
            n_p += 1
            p.wait("vector", pe_p, n_p)
            if kind == "q":
                dstp = qh[0:64, tc * 512:(tc + 1) * 512]
                bia = b1s[0:64, h:h + 1]
            elif kind == "k":
                dstp = kh[0:64, tc * 512:(tc + 1) * 512]
                bia = b1s[64:128, h:h + 1]
            else:
                dstp = Pb[0:64, tc * 512:(tc + 1) * 512]
                bia = b1s[0:64, 16 + h:17 + h]
            p.on("vector", lambda e, bk=bk, dstp=dstp, bia=bia: e.tensor_scalar(
                out=dstp, in0=bk[0:64, :], scalar1=bia, scalar2=None, op0=ALU.add),
                inc=(ev_p, 1))
        p.wait("tensor", ev_p, n_p)
        for vt in range(32):
            bk = banks[2 + (n_vt % 2)]
            if n_vt >= 2:
                p.wait("tensor", ev_vt, n_vt - 1)
            def f(e, bk=bk, vt=vt):
                return e.transpose(bk[:].bitcast(BF16)[0:128, 0:64],
                                   in_=Pb[0:64, vt * 128:(vt + 1) * 128],
                                   identity=ident[0:64, 0:64])
            p.on("tensor", f, inc=(pe_vt, 1))
            n_vt += 1
            p.wait("vector", pe_vt, n_vt)
            p.on("vector", lambda e, bk=bk, vt=vt: e.tensor_copy(
                out=vtm[:, vt * 64:(vt + 1) * 64], in_=bk[:].bitcast(BF16)[0:128, 0:64]),
                inc=(ev_vt, 1))
        p.wait("scalar", ev_vt, n_vt)
        for qi in range(16):
            nmacA = qi // 4 + 1
            W = nmacA * 512 + 2048
            for mi in range(nmacA + 4):
                bk = banks[4 + (n_S % 2)]
                if n_S >= 2:
                    p.wait("tensor", dve_S, n_S - 1)
                kof = mi * 512 if mi < nmacA else 2048 + (mi - nmacA) * 512
                sof = mi * 512
                def f(e, bk=bk, kof=kof, qi=qi):
                    return e.matmul(bk[:], lhsT=qh[0:64, qi * 128:(qi + 1) * 128],
                                    rhs=kh[0:64, kof:kof + 512], start=True, stop=True)
                p.on("tensor", f, inc=(pe_S, 1))
                n_S += 1
                p.wait("vector", pe_S, n_S)
                if mi == 0:
                    p.wait("vector", act_P, n_qt)
                if mi < nmacA:
                    p.on("vector", lambda e, bk=bk, sof=sof: e.tensor_scalar(
                        out=Sb[:, sof:sof + 512], in0=bk[:], scalar1=SCL,
                        scalar2=None, op0=ALU.mult), inc=(dve_S, 1))
                else:
                    def fb(e, bk=bk, sof=sof):
                        e.tensor_scalar(out=Sb[:, sof:sof + 512], in0=bk[:],
                                        scalar1=SCL, scalar2=None, op0=ALU.mult)
                        e.drain()
                        return e.tensor_scalar(out=Sb[:, sof:sof + 512],
                                               in0=Sb[:, sof:sof + 512],
                                               scalar1=flg[:, 0:1], scalar2=flg[:, 1:2],
                                               op0=ALU.mult, op1=ALU.add)
                    p.on("vector", fb, inc=(dve_S, 1))
            r = qi % 4
            dof = (nmacA - 1) * 512

            def fdg(e, r=r, dof=dof, W=W):
                e.tensor_tensor(out=Sb[:, dof:dof + 512], in0=Sb[:, dof:dof + 512],
                                in1=msk[:, r * 512:(r + 1) * 512], op=ALU.mult)
                e.drain()
                e.tensor_tensor(out=Sb[:, dof:dof + 512], in0=Sb[:, dof:dof + 512],
                                in1=mskb[:, r * 512:(r + 1) * 512], op=ALU.add)
                e.drain()
                e.tensor_reduce(out=sst_t[:, 0:1], in_=Sb[:, 0:W], axis=AX.X, op=ALU.max)
                e.drain()
                return e.tensor_scalar(out=sst_t[:, 1:2], in0=sst_t[:, 0:1],
                                       scalar1=-1.0, scalar2=None, op0=ALU.mult)
            p.on("vector", fdg, inc=(dve_P, 1))
            p.wait("scalar", dve_P, dve_P.n)
            p.wait("scalar", pe_V, n_V)
            p.on("scalar", lambda e, W=W: e.activation(
                out=Pb[:, 0:W], in_=Sb[:, 0:W], func=AF.Exp,
                bias=sst_t[:, 1:2], accum_out=sst_t[:, 2:3]), inc=(act_P, 1))
            n_qt += 1
            p.wait("vector", act_P, n_qt)

            def fl(e):
                e.reciprocal(out=sst_t[:, 3:4], in_=sst_t[:, 2:3])
                return e.drain()
            p.on("vector", fl, inc=(dve_P, 1))
            nblk = W // 128
            for g8 in range((nblk + 7) // 8):
                lo = g8 * 8
                hi = min(nblk, lo + 8)
                bk = banks[2 + (n_T % 2)]
                p.wait("tensor", ev_T, ev_T.n)
                if g8 == 0:
                    p.wait("tensor", dve_P, dve_P.n)
                    p.wait("tensor", ev_V, ev_V.n)
                for c in range(lo, hi):
                    def f(e, bk=bk, c=c, lo=lo):
                        return e.transpose(bk[:].bitcast(BF16).rearrange(
                            "p (a r) -> p a r", r=128)[:, c - lo, :],
                            in_=Pb[:, c * 128:(c + 1) * 128], identity=ident[:])
                    if c == hi - 1:
                        p.on("tensor", f, inc=(pe_T, 1))
                    else:
                        p.on("tensor", f)
                n_T += 1
                p.wait("vector", pe_T, n_T)
                p.wait("vector", pe_V, n_V)
                p.on("vector", lambda e, bk=bk, n=hi - lo: e.tensor_copy(
                    out=ptb[:, 0:n * 128], in_=bk[:].bitcast(BF16)[:, 0:n * 128]),
                    inc=(ev_T, 1))
                p.wait("tensor", ev_T, ev_T.n)
                for c in range(lo, hi):
                    vblk = c if c < nmacA * 4 else 16 + (c - nmacA * 4)
                    def f(e, c=c, lo=lo, vblk=vblk, first=(c == 0), last=(c == nblk - 1)):
                        return e.matmul(banks[6][:, 0:64],
                                        lhsT=ptb[:, (c - lo) * 128:(c - lo + 1) * 128],
                                        rhs=vtm[:, vblk * 64:(vblk + 1) * 64],
                                        start=first, stop=last)
                    if c == hi - 1:
                        p.on("tensor", f, inc=(pe_V, 1))
                    else:
                        p.on("tensor", f)
                n_V += 1
            p.wait("vector", pe_V, n_V)
            prow = (h % 2) * 64
            ycol = (h // 2) * T + qi * 128
            p.on("vector", lambda e: e.tensor_scalar(
                out=ptb[:, 0:64], in0=banks[6][:, 0:64],
                scalar1=sst_t[:, 3:4], scalar2=None, op0=ALU.mult), inc=(ev_V, 1))
            p.wait("tensor", ev_V, ev_V.n)
            def ft(e):
                return e.transpose(banks[7][:].bitcast(BF16)[0:64, 0:128],
                                   in_=ptb[:, 0:64], identity=ident[:])
            p.on("tensor", ft, inc=(pe_T, 1))
            n_T += 1
            p.wait("vector", pe_T, n_T)
            p.on("vector", lambda e, prow=prow, ycol=ycol: e.tensor_copy(
                out=yfm[prow:prow + 64, ycol:ycol + 128],
                in_=banks[7][:].bitcast(BF16)[0:64, 0:128]), inc=(ev_T, 1))
    for i in range(NT):
        p.wait("tensor", ev_T, ev_T.n)
        p.wait("tensor", dve_O, dve_O.n)
        for oh in range(2):
            for c in range(ND):
                def f(e, i=i, oh=oh, c=c):
                    return e.matmul(banks[4 + oh][:],
                                    lhsT=yfm[:, c * T + i * 128:c * T + (i + 1) * 128],
                                    rhs=wp[:, c * D + oh * 512:c * D + oh * 512 + 512],
                                    start=(c == 0), stop=False)
                p.on("tensor", f)
            def fb2(e, oh=oh):
                return e.matmul(banks[4 + oh][:], lhsT=ones_row_bf[:],
                                rhs=bp[:, oh * 512:oh * 512 + 512],
                                start=False, stop=True)
            if oh == 1:
                p.on("tensor", fb2, inc=(pe_O, 2))
            else:
                p.on("tensor", fb2)
        sl = (i % 2) * D
        if i >= 2:
            p.wait("sync", st[i % 2], 16 * (i // 2))
        p.dma("sync", xr[:, sl:sl + D], xv[:, i, :], ldr)
        p.wait("vector", ldr, ldr.n)
        p.wait("vector", pe_O, pe_O.n)
        for oh in range(2):
            def f(e, oh=oh, sl=sl):
                return e.tensor_tensor(out=xr[:, sl + oh * 512:sl + oh * 512 + 512],
                                       in0=banks[4 + oh][:],
                                       in1=xr[:, sl + oh * 512:sl + oh * 512 + 512],
                                       op=ALU.add)
            p.on("vector", f, inc=(dve_O, 1))
        p.wait("sync", dve_O, dve_O.n)
        p.dma("sync", y_d.rearrange("(i p) d -> p i d", p=128)[:, i, :],
              xr[:, sl:sl + D], st[i % 2])
    p.wait("sync", st[0])
    p.wait("sync", st[1])
    p.emit_and_close()
    return nc


def _rot(x_full, c):
    s = c // 2
    seq = x_full[s * 4096:(s + 1) * 4096]
    if c % 2 == 0:
        return seq
    return np.concatenate([seq[2048:], seq[0:2048]], axis=0)


def run_l1_attn(x_full, W1, b1, Wp, bp):
    if "l1_attn" not in _NC_CACHE:
        _NC_CACHE["l1_attn"] = build_l1_attn()
    nc = _NC_CACHE["l1_attn"]
    W1 = np.asarray(W1, np.float32)
    b1 = np.asarray(b1, np.float32)
    Wp = np.asarray(Wp, np.float32)
    bp = np.asarray(bp, np.float32)
    w1p = np.ascontiguousarray(W1.reshape(8, 128, 3072).transpose(1, 0, 2).reshape(128, 8 * 3072))
    wpp = np.ascontiguousarray(Wp.reshape(8, 128, 1024).transpose(1, 0, 2).reshape(128, 8 * 1024))
    b1p = np.zeros((128, 32), np.float32)
    for h in range(16):
        b1p[0:64, h] = b1[h * 192:h * 192 + 64]
        b1p[64:128, h] = b1[h * 192 + 64:h * 192 + 128]
        b1p[0:64, 16 + h] = b1[h * 192 + 128:h * 192 + 192]
    bpp = bp.reshape(1, 1024)
    in_maps = []
    for c in range(NCORE):
        flag = 1.0 if c % 2 == 1 else 0.0
        flagv = np.zeros((128, 2), np.float32)
        flagv[:, 0] = flag
        flagv[:, 1] = (flag - 1.0) * 30000.0
        in_maps.append({"x": np.ascontiguousarray(_rot(x_full, c)),
                        "w1p": w1p, "b1p": b1p, "wpp": wpp, "bpp": bpp,
                        "flagv": flagv})
    res = run_bass_kernel_spmd(nc, in_maps, core_ids=list(range(NCORE)))
    return np.concatenate([res.results[c]["y"] for c in range(NCORE)], axis=0)


def build_l2_attn():
    nc = bass.Bass("TRN2", target_bir_lowering=False, debug=False)
    x_d = nc.dram_tensor("x", [S4, D], F32, kind="ExternalInput").ap()
    wa_d = nc.dram_tensor("wap", [128, 8 * 3072], F32, kind="ExternalInput").ap()
    wpr_d = nc.dram_tensor("wprp", [128, 8 * D], F32, kind="ExternalInput").ap()
    flag_d = nc.dram_tensor("flag2", [128, 1], F32, kind="ExternalInput").ap()
    y_d = nc.dram_tensor("y", [T, D], F32, kind="ExternalOutput").ap()
    p = Prog(nc)

    fm = p.sbuf("fm", [128, ND * S4], BF16)
    yfm = p.sbuf("yfm", [128, ND * T], BF16)
    wah = [p.sbuf(f"wah{j}", [128, 8 * 192], BF16) for j in range(2)]
    wpr = p.sbuf("wprs", [128, 8 * D], BF16)
    flg = p.sbuf("flg2", [128, 1], F32)
    qh = p.sbuf("qh", [128, 2048], BF16)
    kh = p.sbuf("kh", [128, S4], BF16)
    eA = p.sbuf("eA", [128, S4], BF16)
    vtm = p.sbuf("vtm", [128, 32 * 64], BF16)
    ktm = p.sbuf("ktm", [128, 32 * 64], BF16)
    Pc = p.sbuf("Pc", [128, 128], BF16)
    tri = p.sbuf("tri", [128, 128], BF16)
    tio = p.sbuf("tio", [128, 128], I32)
    Sst = p.sbuf("Sst", [128, 64], F32)
    Sbf = p.sbuf("Sbf", [128, 64], BF16)
    zst = p.sbuf("zst", [128, 4], F32)
    zbf = p.sbuf("zbf", [128, 1], BF16)
    dr = p.sbuf("dr", [1, 256], F32)
    drb = p.sbuf("drb", [1, 128], BF16)
    bcs = p.sbuf("bcs", [128, 128], BF16)
    ones_col = p.sbuf("ones_c", [128, 1], BF16)
    ones_row = p.sbuf("ones_r", [1, 64], BF16)
    xr = p.sbuf("xr", [128, 2 * D], F32)

    ld = p.sem("ld")
    ldg = p.sem("ldg")
    wl = [p.sem(f"wl{j}") for j in range(2)]
    cst = p.sem("cst")
    pe_p = p.sem("pe_p")
    ev_p = p.sem("ev_p")
    act_e = p.sem("act_e")
    dve_e = p.sem("dve_e")
    pe_vt = p.sem("pe_vt")
    ev_vt = p.sem("ev_vt")
    pe_sd = p.sem("pe_sd")
    dve_sd = p.sem("dve_sd")
    pe_c = p.sem("pe_c")
    dve_c = p.sem("dve_c")
    act_c = p.sem("act_c")
    pe_O = p.sem("pe_O")
    dve_O = p.sem("dve_O")
    ldr = p.sem("ldr")
    st = [p.sem(f"st{j}") for j in range(2)]

    banks = [p.psum(f"bank{j}", [128, 512], F32) for j in range(8)]
    ident, id_done = emit_ident(p)
    xv = x_d.rearrange("(i p) d -> p i d", p=128)

    p.dma("sync", flg[:], flag_d, ld)
    p.on("gpsimd", lambda e: e.dma_start(out=wpr[:], in_=wpr_d), inc=(ldg, 16))

    def fcst(e):
        e.memset(ones_col[:], 1.0)
        e.memset(ones_row[:], 1.0)
        return e.drain()
    p.on("vector", fcst, inc=(cst, 1))

    def ftri(e):
        e.iota(tio[:], pattern=[[-1, 128]], base=0, channel_multiplier=1)
        e.drain()
        return e.tensor_scalar(out=tri[:], in0=tio[:], scalar1=0, scalar2=None,
                               op0=ALU.is_le)
    p.on("gpsimd", ftri, inc=(cst, 1))

    ev_s = emit_ln_tp_stream(p, "ln", xv, fm, ident, id_done, banks[6:8],
                             ntiles=NT4, tcols=S4)

    eB = p.sbuf("eB", [128, S4], BF16)
    n_p = 0
    n_vt = 0
    n_c = 0
    EPS = 1e-6
    for h in range(16):
        sl = h % 2
        if h >= 2:
            p.wait("gpsimd", pe_p, (h - 1) * 20)
        for part in range(3):
            def fdma(e, sl=sl, h=h, part=part):
                return e.dma_start(
                    out=wah[sl][:].rearrange("p (c k) -> p c k", k=192)[:, :, part * 64:(part + 1) * 64],
                    in_=wa_d.rearrange("p (c k) -> p c k", k=3072)[:, :, part * 1024 + h * 64:part * 1024 + (h + 1) * 64])
            p.on("gpsimd", fdma, inc=(wl[sl], 16))
        p.wait("tensor", wl[sl], 48 * (h // 2 + 1))
        if h == 0:
            p.wait("tensor", ev_s, NT4)
            p.wait("tensor", ld, ld.n)
            p.wait("tensor", ldg, 16)
            p.wait("tensor", cst, 2)
        projs = ([("q", 0, tc) for tc in range(4)] +
                 [("k", 64, tc) for tc in range(8)] +
                 [("v", 128, tc) for tc in range(8)])
        for kind, wof, tc in projs:
            bk = banks[n_p % 2]
            if n_p >= 2:
                p.wait("tensor", ev_p, n_p - 1)
            for c in range(ND):
                def f(e, bk=bk, c=c, wof=wof, tc=tc, sl=sl):
                    return e.matmul(bk[0:64, :],
                                    lhsT=wah[sl][:, c * 192 + wof:c * 192 + wof + 64],
                                    rhs=fm[:, c * S4 + tc * 512:c * S4 + (tc + 1) * 512],
                                    start=(c == 0), stop=(c == ND - 1))
                if c == ND - 1:
                    p.on("tensor", f, inc=(pe_p, 1))
                else:
                    p.on("tensor", f)
            n_p += 1
            p.wait("vector", pe_p, n_p)
            if kind == "q":
                dstp = qh[0:64, tc * 512:(tc + 1) * 512]
            elif kind == "k":
                dstp = kh[0:64, tc * 512:(tc + 1) * 512]
            else:
                dstp = eA[0:64, tc * 512:(tc + 1) * 512]
            p.on("vector", lambda e, bk=bk, dstp=dstp: e.tensor_copy(
                out=dstp, in_=bk[0:64, :]), inc=(ev_p, 1))
        p.wait("vector", ev_p, n_p)
        p.wait("vector", act_e, act_e.n)

        def felu_min(e):
            e.tensor_scalar(out=eB[0:64, 0:2048], in0=qh[0:64, :],
                            scalar1=0.0, scalar2=None, op0=ALU.min)
            e.drain()
            e.tensor_scalar(out=eB[0:64, 2048:2048 + S4 - 2048], in0=kh[0:64, 0:2048],
                            scalar1=0.0, scalar2=None, op0=ALU.min) if False else None
            return e.nop()
        def felu1(e):
            e.tensor_scalar(out=eB[0:64, 0:2048], in0=qh[0:64, :],
                            scalar1=0.0, scalar2=None, op0=ALU.min)
            e.drain()
            e.tensor_scalar(out=qh[0:64, :], in0=qh[0:64, :],
                            scalar1=0.0, scalar2=None, op0=ALU.max)
            return e.drain()
        p.on("vector", felu1, inc=(dve_e, 1))
        p.wait("scalar", dve_e, dve_e.n)
        p.on("scalar", lambda e: e.activation(out=eB[0:64, 0:2048],
                                              in_=eB[0:64, 0:2048], func=AF.Exp),
             inc=(act_e, 1))
        p.wait("vector", act_e, act_e.n)

        def felu2(e):
            e.tensor_tensor(out=qh[0:64, :], in0=qh[0:64, :], in1=eB[0:64, 0:2048],
                            op=ALU.add)
            e.drain()
            e.tensor_scalar(out=eB[0:64, :], in0=kh[0:64, :],
                            scalar1=0.0, scalar2=None, op0=ALU.min)
            e.drain()
            e.tensor_scalar(out=kh[0:64, :], in0=kh[0:64, :],
                            scalar1=0.0, scalar2=None, op0=ALU.max)
            return e.drain()
        p.on("vector", felu2, inc=(dve_e, 1))
        p.wait("scalar", dve_e, dve_e.n)
        p.on("scalar", lambda e: e.activation(out=eB[0:64, :], in_=eB[0:64, :],
                                              func=AF.Exp), inc=(act_e, 1))
        p.wait("vector", act_e, act_e.n)
        p.on("vector", lambda e: e.tensor_tensor(out=kh[0:64, :], in0=kh[0:64, :],
                                                 in1=eB[0:64, :], op=ALU.add),
             inc=(dve_e, 1))
        p.wait("tensor", dve_e, dve_e.n)
        for src_buf, dst_buf, nb in ((eA, vtm, 32), (kh, ktm, 32)):
            for vt in range(nb):
                bk = banks[2 + (n_vt % 2)]
                if n_vt >= 2:
                    p.wait("tensor", ev_vt, n_vt - 1)
                def f(e, bk=bk, vt=vt, src_buf=src_buf):
                    return e.transpose(bk[:].bitcast(BF16)[0:128, 0:64],
                                       in_=src_buf[0:64, vt * 128:(vt + 1) * 128],
                                       identity=ident[0:64, 0:64])
                p.on("tensor", f, inc=(pe_vt, 1))
                n_vt += 1
                p.wait("vector", pe_vt, n_vt)
                p.on("vector", lambda e, bk=bk, vt=vt, dst_buf=dst_buf: e.tensor_copy(
                    out=dst_buf[:, vt * 64:(vt + 1) * 64],
                    in_=bk[:].bitcast(BF16)[0:128, 0:64]), inc=(ev_vt, 1))
        p.wait("tensor", ev_vt, n_vt)
        p.wait("tensor", dve_c, dve_c.n)
        for cc in range(16):
            def f(e, cc=cc, first=(cc == 0), last=(cc == 15)):
                return e.matmul(banks[4][0:64, 0:64],
                                lhsT=ktm[:, (16 + cc) * 64:(17 + cc) * 64],
                                rhs=vtm[:, (16 + cc) * 64:(17 + cc) * 64],
                                start=first, stop=last)
            if cc == 15:
                p.on("tensor", f, inc=(pe_sd, 1))
            else:
                p.on("tensor", f)
        p.wait("vector", pe_sd, pe_sd.n)

        def fseed(e):
            e.tensor_scalar(out=Sst[0:64, :], in0=banks[4][0:64, 0:64],
                            scalar1=flg[0:64, 0:1], scalar2=None, op0=ALU.mult)
            e.drain()
            e.tensor_copy(out=Sbf[0:64, :], in_=Sst[0:64, :])
            e.drain()
            e.tensor_reduce(out=zst[0:64, 1:2], in_=kh[0:64, 2048:4096], axis=AX.X,
                            op=ALU.add)
            e.drain()
            e.tensor_scalar(out=zst[0:64, 0:1], in0=zst[0:64, 1:2],
                            scalar1=flg[0:64, 0:1], scalar2=None, op0=ALU.mult)
            e.drain()
            e.tensor_copy(out=zbf[0:64, :], in_=zst[0:64, 0:1])
            return e.drain()
        p.on("vector", fseed, inc=(dve_sd, 1))
        p.wait("tensor", dve_sd, dve_sd.n)
        for cc in range(16):
            p.wait("tensor", dve_c, dve_c.n)
            def f1(e, cc=cc):
                return e.matmul(banks[4][:, 0:128],
                                lhsT=kh[0:64, cc * 128:(cc + 1) * 128],
                                rhs=qh[0:64, cc * 128:(cc + 1) * 128],
                                start=True, stop=True)
            p.on("tensor", f1, inc=(pe_c, 1))
            n_c += 1
            p.wait("vector", pe_c, n_c)
            p.on("vector", lambda e: e.tensor_tensor(
                out=Pc[:], in0=banks[4][:, 0:128], in1=tri[:], op=ALU.mult),
                inc=(dve_c, 1))
            p.wait("tensor", dve_c, dve_c.n)
            def f2(e, cc=cc):
                e.matmul(banks[5][0:1, 0:128], lhsT=ones_col[:], rhs=Pc[:],
                         start=True, stop=False)
                e.matmul(banks[5][0:1, 0:128], lhsT=zbf[0:64, :],
                         rhs=qh[0:64, cc * 128:(cc + 1) * 128], start=False, stop=True)
                e.matmul(banks[6][0:64, 0:128], lhsT=vtm[:, cc * 64:(cc + 1) * 64],
                         rhs=Pc[:], start=True, stop=False)
                return e.matmul(banks[6][0:64, 0:128], lhsT=Sbf[0:64, :],
                                rhs=qh[0:64, cc * 128:(cc + 1) * 128],
                                start=False, stop=True)
            p.on("tensor", f2, inc=(pe_c, 1))
            n_c += 1
            p.wait("vector", pe_c, n_c)

            def fden(e):
                e.tensor_scalar(out=dr[:, 0:128], in0=banks[5][0:1, 0:128],
                                scalar1=EPS, scalar2=None, op0=ALU.add)
                e.drain()
                e.reciprocal(out=dr[:, 128:256], in_=dr[:, 0:128])
                e.drain()
                e.tensor_copy(out=drb[:], in_=dr[:, 128:256])
                return e.drain()
            p.on("vector", fden, inc=(dve_c, 1))
            p.wait("tensor", dve_c, dve_c.n)
            def f3(e):
                return e.matmul(banks[7][0:64, 0:128], lhsT=ones_row[:], rhs=drb[:],
                                start=True, stop=True)
            p.on("tensor", f3, inc=(pe_c, 1))
            n_c += 1
            p.wait("vector", pe_c, n_c)
            ycol = (h // 2) * T + cc * 128
            prow = (h % 2) * 64

            def fy(e, ycol=ycol, prow=prow):
                e.tensor_copy(out=bcs[0:64, :], in_=banks[7][0:64, 0:128])
                e.drain()
                return e.tensor_tensor(out=yfm[prow:prow + 64, ycol:ycol + 128],
                                       in0=banks[6][0:64, 0:128], in1=bcs[0:64, :],
                                       op=ALU.mult)
            p.on("vector", fy, inc=(dve_c, 1))
            p.wait("tensor", dve_c, dve_c.n)
            def f4(e, cc=cc):
                return e.matmul(banks[4][0:64, 0:64],
                                lhsT=ktm[:, cc * 64:(cc + 1) * 64],
                                rhs=vtm[:, cc * 64:(cc + 1) * 64],
                                start=True, stop=True)
            p.on("tensor", f4, inc=(pe_c, 1))
            n_c += 1
            p.wait("vector", pe_c, n_c)

            def fupd(e, cc=cc):
                e.tensor_tensor(out=Sst[0:64, :], in0=banks[4][0:64, 0:64],
                                in1=Sst[0:64, :], op=ALU.add)
                e.drain()
                e.tensor_copy(out=Sbf[0:64, :], in_=Sst[0:64, :])
                e.drain()
                e.tensor_reduce(out=zst[0:64, 2:3], in_=kh[0:64, cc * 128:(cc + 1) * 128],
                                axis=AX.X, op=ALU.add)
                e.drain()
                e.tensor_tensor(out=zst[0:64, 0:1], in0=zst[0:64, 2:3],
                                in1=zst[0:64, 0:1], op=ALU.add)
                e.drain()
                e.tensor_copy(out=zbf[0:64, :], in_=zst[0:64, 0:1])
                return e.drain()
            p.on("vector", fupd, inc=(dve_c, 1))
    for i in range(NT):
        p.wait("tensor", dve_c, dve_c.n)
        p.wait("tensor", dve_O, dve_O.n)
        for oh in range(2):
            for c in range(ND):
                def f(e, i=i, oh=oh, c=c, last=(c == ND - 1)):
                    return e.matmul(banks[4 + oh][:],
                                    lhsT=yfm[:, c * T + i * 128:c * T + (i + 1) * 128],
                                    rhs=wpr[:, c * D + oh * 512:c * D + oh * 512 + 512],
                                    start=(c == 0), stop=last)
                if c == ND - 1:
                    p.on("tensor", f, inc=(pe_O, 1))
                else:
                    p.on("tensor", f)
        sl = (i % 2) * D
        if i >= 2:
            p.wait("sync", st[i % 2], 16 * (i // 2))
        p.dma("sync", xr[:, sl:sl + D], xv[:, i, :], ldr)
        p.wait("vector", ldr, ldr.n)
        p.wait("vector", pe_O, pe_O.n)
        for oh in range(2):
            def f(e, oh=oh, sl=sl):
                return e.tensor_tensor(out=xr[:, sl + oh * 512:sl + oh * 512 + 512],
                                       in0=banks[4 + oh][:],
                                       in1=xr[:, sl + oh * 512:sl + oh * 512 + 512],
                                       op=ALU.add)
            p.on("vector", f, inc=(dve_O, 1))
        p.wait("sync", dve_O, dve_O.n)
        p.dma("sync", y_d.rearrange("(i p) d -> p i d", p=128)[:, i, :],
              xr[:, sl:sl + D], st[i % 2])
    p.wait("sync", st[0])
    p.wait("sync", st[1])
    p.emit_and_close()
    return nc


def run_l2_attn(x_full, w_attn, w_proj):
    if "l2_attn" not in _NC_CACHE:
        _NC_CACHE["l2_attn"] = build_l2_attn()
    nc = _NC_CACHE["l2_attn"]
    w_attn = np.asarray(w_attn, np.float32)
    w_proj = np.asarray(w_proj, np.float32)
    wap = np.ascontiguousarray(w_attn.reshape(8, 128, 3072).transpose(1, 0, 2).reshape(128, 8 * 3072))
    wprp = np.ascontiguousarray(w_proj.reshape(8, 128, 1024).transpose(1, 0, 2).reshape(128, 8 * 1024))
    in_maps = []
    for c in range(NCORE):
        flag = np.full((128, 1), 1.0 if c % 2 == 1 else 0.0, np.float32)
        in_maps.append({"x": np.ascontiguousarray(_rot(x_full, c)),
                        "wap": wap, "wprp": wprp, "flag2": flag})
    res = run_bass_kernel_spmd(nc, in_maps, core_ids=list(range(NCORE)))
    return np.concatenate([res.results[c]["y"] for c in range(NCORE)], axis=0)


def build_l2_gate():
    nc = bass.Bass("TRN2", target_bir_lowering=False, debug=False)
    x_d = nc.dram_tensor("x", [T, D], F32, kind="ExternalInput").ap()
    g_d = nc.dram_tensor("gTp", [128, 8 * 32], F32, kind="ExternalInput").ap()
    lo_d = nc.dram_tensor("logits", [T, 32], F32, kind="ExternalOutput").ap()
    p = Prog(nc)
    fm = p.sbuf("fm", [128, ND * T], BF16)
    gT = p.sbuf("gTs", [128, 8 * 32], BF16)
    lob = p.sbuf("lob", [128, 32], F32)
    ld = p.sem("ld")
    ldg = p.sem("ldg")
    pe_g = p.sem("pe_g")
    dve_g = p.sem("dve_g")
    st = p.sem("st")
    banks = [p.psum(f"bank{j}", [128, 512], F32) for j in range(8)]
    ident, id_done = emit_ident(p)
    xv = x_d.rearrange("(i p) d -> p i d", p=128)
    p.on("gpsimd", lambda e: e.dma_start(out=gT[:], in_=g_d), inc=(ldg, 16))
    ev_s = emit_ln_tp_stream(p, "ln", xv, fm, ident, id_done, banks[6:8])
    p.wait("tensor", ev_s, NT)
    p.wait("tensor", ldg, 16)
    for i in range(NT):
        bk = banks[i % 2]
        if i >= 2:
            p.wait("tensor", dve_g, i - 1)
        for c in range(ND):
            def f(e, bk=bk, c=c, i=i):
                return e.matmul(bk[:, 0:32], lhsT=fm[:, c * T + i * 128:c * T + (i + 1) * 128],
                                rhs=gT[:, c * 32:(c + 1) * 32],
                                start=(c == 0), stop=(c == ND - 1))
            if c == ND - 1:
                p.on("tensor", f, inc=(pe_g, 1))
            else:
                p.on("tensor", f)
        p.wait("vector", pe_g, i + 1)
        p.wait("vector", st, 16 * i)
        p.on("vector", lambda e, bk=bk: e.tensor_copy(out=lob[:], in_=bk[:, 0:32]),
             inc=(dve_g, 1))
        p.wait("sync", dve_g, i + 1)
        p.dma("sync", lo_d[i * 128:(i + 1) * 128, :], lob[:], st)
    p.wait("sync", st)
    p.emit_and_close()
    return nc


def build_l2_ffn():
    nc = bass.Bass("TRN2", target_bir_lowering=False, debug=False)
    x_d = nc.dram_tensor("x", [T, D], F32, kind="ExternalInput").ap()
    lo_d = nc.dram_tensor("logits", [16384, 32], F32, kind="ExternalInput").ap()
    gu_d = nc.dram_tensor("gup", [128, 32 * 8 * 1536], F32, kind="ExternalInput").ap()
    dn_d = nc.dram_tensor("dnp", [128, 32 * 6 * 1024], F32, kind="ExternalInput").ap()
    oix_d = nc.dram_tensor("oix", [128, 16], U32, kind="ExternalInput").ap()
    y_d = nc.dram_tensor("y", [T, D], F32, kind="ExternalOutput").ap()
    wt_dram = nc.dram_tensor("wts", [16384, 32], F32, kind="Internal").ap()
    p = Prog(nc)

    fm = p.sbuf("fm", [128, ND * T], BF16)
    acc = p.sbuf("acc", [128, NT * D], BF16)
    wgu = [p.sbuf(f"wgu{j}", [128, 8 * 1536], BF16) for j in range(2)]
    wdn = p.sbuf("wdn", [128, 6 * 1024], BF16)
    hfm = p.sbuf("hfm", [128, 6 * T], BF16)
    gb = p.sbuf("gb", [128, 512], BF16)
    wown = p.sbuf("wown", [128, 16 * 32], F32)
    oix = p.sbuf("oixs", [128, 16], U32)
    trib = p.sbuf("trib", [128, 128], BF16)
    tio = p.sbuf("tio2", [128, 128], I32)
    offs = p.sbuf("offs", [1, 32], BF16)
    ones1 = p.sbuf("ones1", [1, 128], BF16)
    ones_col = p.sbuf("onesc", [128, 1], BF16)
    xr = p.sbuf("xr", [128, 2 * D], F32)

    ld = p.sem("ld")
    wl = [p.sem(f"wl{j}") for j in range(2)]
    wld = p.sem("wld")
    cst = p.sem("cst")
    dve_r = p.sem("dve_r")
    act_r = p.sem("act_r")
    pe_cs = p.sem("pe_cs")
    dve_cs = p.sem("dve_cs")
    st_w = p.sem("st_w")
    gix = p.sem("gix")
    pe_gu = p.sem("pe_gu")
    act_g = p.sem("act_g")
    dve_h = p.sem("dve_h")
    pe_d = p.sem("pe_d")
    dve_a = p.sem("dve_a")
    ldr = p.sem("ldr")
    st = [p.sem(f"st{j}") for j in range(2)]

    banks = [p.psum(f"bank{j}", [128, 512], F32) for j in range(8)]
    ident, id_done = emit_ident(p)
    xv = x_d.rearrange("(i p) d -> p i d", p=128)
    p.dma("sync", oix[:], oix_d, ld)

    def fz(e):
        e.memset(acc[:], 0.0)
        e.memset(ones1[:], 1.0)
        e.memset(ones_col[:], 1.0)
        return e.drain()
    p.on("vector", fz, inc=(cst, 1))

    def ftri(e):
        e.iota(tio[:], pattern=[[-1, 128]], base=0, channel_multiplier=1)
        e.drain()
        return e.tensor_scalar(out=trib[:], in0=tio[:], scalar1=0, scalar2=None,
                               op0=ALU.is_le)
    p.on("gpsimd", ftri, inc=(cst, 1))

    lg_g = nc.sbuf_tensor("lg", [128, 128 * 32], F32)
    lg = lg_g.__enter__()
    rP_g = nc.sbuf_tensor("rP", [128, 128 * 32], F32)
    rP = rP_g.__enter__()
    rM_g = nc.sbuf_tensor("rM", [128, 128 * 32], BF16)
    rM = rM_g.__enter__()
    rS_g = nc.sbuf_tensor("rS", [128, 256], F32)
    rS = rS_g.__enter__()
    rp_g = nc.sbuf_tensor("rp", [128, 512], F32)
    rp = rp_g.__enter__()
    p.dma("sync", lg[:].rearrange("p (a e) -> p a e", e=32),
          lo_d.rearrange("(a p) e -> p a e", p=128), ld)
    p.wait("vector", ld, ld.n)
    p.wait("vector", cst, 1)
    for g in range(8):
        lo = g * 16 * 32
        lgv = lg[:, lo:lo + 512].rearrange("p (a e) -> p a e", e=32)
        pv = rP[:, lo:lo + 512].rearrange("p (a e) -> p a e", e=32)
        mv = rM[:, lo:lo + 512].rearrange("p (a e) -> p a e", e=32)
        sv = rS[:, 0:256].rearrange("p (a k) -> p a k", k=16)
        rpv = rp[:].rearrange("p (a e) -> p a e", e=32)

        def fr1(e, lgv=lgv, pv=pv, sv=sv):
            e.tensor_reduce(out=sv[:, :, 0:1].squeeze(2), in_=lgv, axis=AX.X, op=ALU.max)
            e.drain()
            e.tensor_tensor(out=pv, in0=lgv, in1=sv[:, :, 0:1].broadcast_to([128, 16, 32]),
                            op=ALU.subtract)
            return e.drain()
        p.on("vector", fr1, inc=(dve_r, 1))
        p.wait("scalar", dve_r, dve_r.n)
        p.on("scalar", lambda e, pv=pv: e.activation(out=pv, in_=pv, func=AF.Exp),
             inc=(act_r, 1))
        p.wait("vector", act_r, act_r.n)

        def fr2(e, lgv=lgv, pv=pv, mv=mv, sv=sv, rpv=rpv):
            e.tensor_reduce(out=sv[:, :, 1:2].squeeze(2), in_=pv, axis=AX.X, op=ALU.add)
            e.drain()
            e.reciprocal(out=sv[:, :, 2:3].squeeze(2), in_=sv[:, :, 1:2].squeeze(2))
            e.drain()
            e.tensor_tensor(out=pv, in0=pv, in1=sv[:, :, 2:3].broadcast_to([128, 16, 32]),
                            op=ALU.mult)
            e.drain()
            e.tensor_copy(out=lgv, in_=pv)
            e.drain()
            e.memset(mv, 0.0)
            e.drain()
            for it in range(8):
                e.tensor_reduce(out=sv[:, :, 3:4].squeeze(2), in_=lgv, axis=AX.X, op=ALU.max)
                e.drain()
                e.tensor_tensor(out=rpv, in0=lgv, in1=sv[:, :, 3:4].broadcast_to([128, 16, 32]),
                                op=ALU.is_ge)
                e.drain()
                e.tensor_tensor(out=mv, in0=mv, in1=rpv, op=ALU.max)
                e.drain()
                e.tensor_scalar(out=rpv, in0=rpv, scalar1=-1.0, scalar2=1.0,
                                op0=ALU.mult, op1=ALU.add)
                e.drain()
                e.tensor_tensor(out=lgv, in0=lgv, in1=rpv, op=ALU.mult)
                e.drain()
            e.tensor_tensor(out=pv, in0=pv, in1=mv, op=ALU.mult)
            e.drain()
            e.tensor_reduce(out=sv[:, :, 4:5].squeeze(2), in_=pv, axis=AX.X, op=ALU.add)
            e.drain()
            e.reciprocal(out=sv[:, :, 5:6].squeeze(2), in_=sv[:, :, 4:5].squeeze(2))
            e.drain()
            e.tensor_tensor(out=pv, in0=pv, in1=sv[:, :, 5:6].broadcast_to([128, 16, 32]),
                            op=ALU.mult)
            return e.drain()
        p.on("vector", fr2, inc=(dve_r, 1))
    p.wait("tensor", dve_r, dve_r.n)
    p.wait("tensor", cst, 2)
    for a in range(128):
        p.wait("vector", pe_cs, pe_cs.n)
        if a == 0:
            def fo(e):
                e.memset(offs[:], 0.0)
                return e.drain()
            p.on("vector", fo, inc=(dve_cs, 1))
        else:
            p.on("vector", lambda e: e.tensor_copy(out=offs[:], in_=banks[1][0:1, 0:32]),
                 inc=(dve_cs, 1))
        p.wait("tensor", dve_cs, dve_cs.n)
        mslice = rM[:, a * 32:(a + 1) * 32]

        def fmm(e, a=a, mslice=mslice):
            e.matmul(banks[0][:, 0:32], lhsT=trib[:], rhs=mslice, start=True, stop=False)
            e.matmul(banks[0][:, 0:32], lhsT=ones1[:], rhs=offs[:], start=False, stop=True)
            return e.matmul(banks[1][0:1, 0:32], lhsT=ones_col[:], rhs=mslice,
                            start=(a == 0), stop=True)
        p.on("tensor", fmm, inc=(pe_cs, 1))
        p.wait("vector", pe_cs, pe_cs.n)

        def fk(e, a=a):
            e.tensor_scalar(out=rp[:, 0:32], in0=banks[0][:, 0:32], scalar1=1024.5,
                            scalar2=None, op0=ALU.is_le)
            e.drain()
            return e.tensor_tensor(out=lg[:, a * 32:(a + 1) * 32],
                                   in0=rP[:, a * 32:(a + 1) * 32], in1=rp[:, 0:32],
                                   op=ALU.mult)
        p.on("vector", fk, inc=(dve_cs, 1))
    p.wait("sync", dve_cs, dve_cs.n)
    p.dma("sync", wt_dram.rearrange("(a p) e -> p a e", p=128),
          lg[:].rearrange("p (a e) -> p a e", e=32), st_w)
    p.wait("gpsimd", st_w, 16)
    for i in range(NT):
        def fg(e, i=i):
            return e.indirect_dma_start(
                out=wown[:, i * 32:(i + 1) * 32], out_offset=None,
                in_=wt_dram, in_offset=IndirectOffsetOnAxis(ap=oix[:, i:i + 1], axis=0))
        p.on("gpsimd", fg, inc=(gix, 16))
    rp_g.__exit__(None, None, None)
    rS_g.__exit__(None, None, None)
    rM_g.__exit__(None, None, None)
    rP_g.__exit__(None, None, None)
    lg_g.__exit__(None, None, None)

    ev_s = emit_ln_tp_stream(p, "ln", xv, fm, ident, id_done, banks[6:8])

    guv = gu_d.rearrange("p (x k) -> p x k", k=8 * 1536)
    dnv = dn_d.rearrange("p (x k) -> p x k", k=6 * 1024)
    n_gu = 0
    n_d = 0
    for e_i in range(32):
        sl = e_i % 2
        if e_i >= 2:
            p.wait("gpsimd", pe_gu, 24 * (e_i - 1))
        p.on("gpsimd", lambda e, sl=sl, e_i=e_i: e.dma_start(
            out=wgu[sl][:], in_=guv[:, e_i, :]), inc=(wl[sl], 16))
        p.wait("gpsimd", pe_d, n_d)
        p.on("gpsimd", lambda e, e_i=e_i: e.dma_start(
            out=wdn[:], in_=dnv[:, e_i, :]), inc=(wld, 16))
        p.wait("tensor", wl[sl], 16 * (e_i // 2 + 1))
        if e_i == 0:
            p.wait("tensor", ev_s, NT)
            p.wait("tensor", gix, gix.n)
        for fg4 in range(6):
            for tc in range(4):
                bkA = banks[0 + (n_gu % 2) * 2]
                bkB = banks[1 + (n_gu % 2) * 2]
                if n_gu >= 2:
                    p.wait("tensor", dve_h, n_gu - 1)
                for c in range(ND):
                    def f(e, bkA=bkA, c=c, fg4=fg4, tc=tc, sl=sl):
                        return e.matmul(bkA[:],
                                        lhsT=wgu[sl][:, c * 1536 + fg4 * 128:c * 1536 + (fg4 + 1) * 128],
                                        rhs=fm[:, c * T + tc * 512:c * T + (tc + 1) * 512],
                                        start=(c == 0), stop=(c == ND - 1))
                    p.on("tensor", f)
                for c in range(ND):
                    def f(e, bkB=bkB, c=c, fg4=fg4, tc=tc, sl=sl):
                        return e.matmul(bkB[:],
                                        lhsT=wgu[sl][:, c * 1536 + 768 + fg4 * 128:c * 1536 + 768 + (fg4 + 1) * 128],
                                        rhs=fm[:, c * T + tc * 512:c * T + (tc + 1) * 512],
                                        start=(c == 0), stop=(c == ND - 1))
                    if c == ND - 1:
                        p.on("tensor", f, inc=(pe_gu, 1))
                    else:
                        p.on("tensor", f)
                n_gu += 1
                p.wait("scalar", pe_gu, n_gu)
                if n_gu >= 2:
                    p.wait("scalar", dve_h, n_gu - 1)
                p.on("scalar", lambda e, bkA=bkA: e.activation(
                    out=gb[:], in_=bkA[:], func=AF.Silu), inc=(act_g, 1))
                p.wait("vector", act_g, n_gu)
                dsth = hfm[:, fg4 * T + tc * 512:fg4 * T + (tc + 1) * 512]
                p.on("vector", lambda e, bkB=bkB, dsth=dsth: e.tensor_tensor(
                    out=dsth, in0=gb[:], in1=bkB[:], op=ALU.mult), inc=(dve_h, 1))
        p.wait("tensor", dve_h, dve_h.n)
        p.wait("tensor", wld, 16 * (e_i + 1))
        for ti in range(NT):
            p.wait("tensor", dve_a, dve_a.n)
            for oh in range(2):
                for kc in range(6):
                    def f(e, ti=ti, oh=oh, kc=kc):
                        return e.matmul(banks[4 + oh][:],
                                        lhsT=hfm[:, kc * T + ti * 128:kc * T + (ti + 1) * 128],
                                        rhs=wdn[:, kc * 1024 + oh * 512:kc * 1024 + oh * 512 + 512],
                                        start=(kc == 0), stop=(kc == 5))
                    if kc == 5 and oh == 1:
                        p.on("tensor", f, inc=(pe_d, 1))
                    else:
                        p.on("tensor", f)
            n_d += 1
            p.wait("vector", pe_d, n_d)
            wv = wown[:, ti * 32 + e_i:ti * 32 + e_i + 1]
            for oh in range(2):
                def f(e, ti=ti, oh=oh, wv=wv):
                    return e.scalar_tensor_tensor(
                        out=acc[:, ti * D + oh * 512:ti * D + oh * 512 + 512],
                        in0=banks[4 + oh][:], scalar=wv,
                        in1=acc[:, ti * D + oh * 512:ti * D + oh * 512 + 512],
                        op0=ALU.mult, op1=ALU.add)
                p.on("vector", f, inc=(dve_a, 1))
    for i in range(NT):
        sl = (i % 2) * D
        if i >= 2:
            p.wait("sync", st[i % 2], 16 * (i // 2))
        p.dma("sync", xr[:, sl:sl + D], xv[:, i, :], ldr)
        p.wait("vector", ldr, ldr.n)
        p.wait("vector", dve_a, dve_a.n)
        for oh in range(2):
            def f(e, i=i, oh=oh, sl=sl):
                return e.scalar_tensor_tensor(
                    out=xr[:, sl + oh * 512:sl + oh * 512 + 512],
                    in0=acc[:, i * D + oh * 512:i * D + oh * 512 + 512], scalar=0.5,
                    in1=xr[:, sl + oh * 512:sl + oh * 512 + 512],
                    op0=ALU.mult, op1=ALU.add)
            p.on("vector", f, inc=(dve_a, 1))
        p.wait("sync", dve_a, dve_a.n)
        p.dma("sync", y_d.rearrange("(i p) d -> p i d", p=128)[:, i, :],
              xr[:, sl:sl + D], st[i % 2])
    p.wait("sync", st[0])
    p.wait("sync", st[1])
    p.emit_and_close()
    return nc


def run_l2_ffn(x_full, gate_weight, gate_up_proj, down_proj):
    if "l2_gate" not in _NC_CACHE:
        _NC_CACHE["l2_gate"] = build_l2_gate()
    if "l2_ffn" not in _NC_CACHE:
        _NC_CACHE["l2_ffn"] = build_l2_ffn()
    gw = np.asarray(gate_weight, np.float32)
    gup = np.asarray(gate_up_proj, np.float32)
    dnp = np.asarray(down_proj, np.float32)
    gTp = np.ascontiguousarray(gw.T.reshape(8, 128, 32).transpose(1, 0, 2).reshape(128, 8 * 32))
    in_maps = [{"x": np.ascontiguousarray(x_full[c * T:(c + 1) * T]), "gTp": gTp}
               for c in range(NCORE)]
    res = run_bass_kernel_spmd(_NC_CACHE["l2_gate"], in_maps, core_ids=list(range(NCORE)))
    logits = np.concatenate([res.results[c]["logits"] for c in range(NCORE)], axis=0)
    gupT = np.zeros((128, 32 * 8 * 1536), np.float32)
    dnpT = np.zeros((128, 32 * 6 * 1024), np.float32)
    for e_i in range(32):
        wt = gup[e_i].T
        gupT[:, e_i * 8 * 1536:(e_i + 1) * 8 * 1536] = \
            wt.reshape(8, 128, 1536).transpose(1, 0, 2).reshape(128, 8 * 1536)
        wd = dnp[e_i].T
        dnpT[:, e_i * 6 * 1024:(e_i + 1) * 6 * 1024] = \
            wd.reshape(6, 128, 1024).transpose(1, 0, 2).reshape(128, 6 * 1024)
    in_maps = []
    for c in range(NCORE):
        oixv = np.ascontiguousarray(
            np.arange(c * T, (c + 1) * T, dtype=np.uint32).reshape(16, 128).T)
        in_maps.append({"x": np.ascontiguousarray(x_full[c * T:(c + 1) * T]),
                        "logits": logits, "gup": gupT, "dnp": dnpT, "oix": oixv})
    res = run_bass_kernel_spmd(_NC_CACHE["l2_ffn"], in_maps, core_ids=list(range(NCORE)))
    return np.concatenate([res.results[c]["y"] for c in range(NCORE)], axis=0)


def build_l3_attn():
    nc = bass.Bass("TRN2", target_bir_lowering=False, debug=False)
    x_d = nc.dram_tensor("x", [S4, D], F32, kind="ExternalInput").ap()
    wq_d = nc.dram_tensor("wqp", [128, 8 * 3072], F32, kind="ExternalInput").ap()
    wo_d = nc.dram_tensor("wop", [128, 8 * D], F32, kind="ExternalInput").ap()
    flag_d = nc.dram_tensor("flag3", [128, 2], F32, kind="ExternalInput").ap()
    y_d = nc.dram_tensor("y", [T, D], F32, kind="ExternalOutput").ap()
    p = Prog(nc)

    fm = p.sbuf("fm", [128, ND * S4], BF16)
    yfm = p.sbuf("yfm", [128, ND * T], BF16)
    wqh = [p.sbuf(f"wqh{j}", [128, 8 * 192], BF16) for j in range(2)]
    wo = p.sbuf("wos", [128, 8 * D], BF16)
    flg = p.sbuf("flg3", [128, 2], F32)
    qh = p.sbuf("qh", [128, 2048], BF16)
    kh = p.sbuf("kh", [128, S4], BF16)
    vtm = p.sbuf("vtm", [128, 32 * 64], BF16)
    Sb = p.sbuf("Sb", [128, 1024], F32)
    Pb = p.sbuf("Pb", [128, 1024], BF16)
    ptb = p.sbuf("ptb", [128, 8 * 128], BF16)
    mskd = p.sbuf("mskd", [128, 128], BF16)
    mskdb = p.sbuf("mskdb", [128, 128], BF16)
    mio = p.sbuf("mio", [128, 128], I32)
    sst_t = p.sbuf("sst_t", [128, 8], F32)
    vstage = p.sbuf("vstage", [128, S4], BF16)
    xr = p.sbuf("xr", [128, 2 * D], F32)

    ld = p.sem("ld")
    ldg = p.sem("ldg")
    wl = [p.sem(f"wl{j}") for j in range(2)]
    cst = p.sem("cst")
    pe_p = p.sem("pe_p")
    ev_p = p.sem("ev_p")
    pe_vt = p.sem("pe_vt")
    ev_vt = p.sem("ev_vt")
    pe_S = p.sem("pe_S")
    dve_S = p.sem("dve_S")
    act_P = p.sem("act_P")
    dve_P = p.sem("dve_P")
    pe_T = p.sem("pe_T")
    ev_T = p.sem("ev_T")
    pe_V = p.sem("pe_V")
    ev_V = p.sem("ev_V")
    pe_O = p.sem("pe_O")
    dve_O = p.sem("dve_O")
    ldr = p.sem("ldr")
    st = [p.sem(f"st{j}") for j in range(2)]

    banks = [p.psum(f"bank{j}", [128, 512], F32) for j in range(8)]
    ident, id_done = emit_ident(p)
    xv = x_d.rearrange("(i p) d -> p i d", p=128)

    p.dma("sync", flg[:], flag_d, ld)
    p.on("gpsimd", lambda e: e.dma_start(out=wo[:], in_=wo_d), inc=(ldg, 16))

    def fmsk(e):
        e.iota(mio[:], pattern=[[1, 128]], base=0, channel_multiplier=-1)
        e.drain()
        e.tensor_scalar(out=mskd[:], in0=mio[:], scalar1=0, scalar2=None, op0=ALU.is_le)
        e.drain()
        e.tensor_scalar(out=mskdb[:], in0=mskd[:], scalar1=-1.0, scalar2=1.0e6,
                        op0=ALU.add, op1=ALU.mult)
        return e.drain()
    p.on("gpsimd", fmsk, inc=(cst, 1))

    ev_s = emit_ln_tp_stream(p, "ln", xv, fm, ident, id_done, banks[6:8],
                             ntiles=NT4, tcols=S4)

    SCL = 1.0 / 8.0
    n_p = 0
    n_vt = 0
    n_S = 0
    n_qt = 0
    n_T = 0
    n_V = 0
    for h in range(16):
        sl = h % 2
        if h >= 2:
            p.wait("gpsimd", pe_p, (h - 1) * 20)
        for part in range(3):
            def fdma(e, sl=sl, h=h, part=part):
                return e.dma_start(
                    out=wqh[sl][:].rearrange("p (c k) -> p c k", k=192)[:, :, part * 64:(part + 1) * 64],
                    in_=wq_d.rearrange("p (c k) -> p c k", k=3072)[:, :, part * 1024 + h * 64:part * 1024 + (h + 1) * 64])
            p.on("gpsimd", fdma, inc=(wl[sl], 16))
        p.wait("tensor", wl[sl], 48 * (h // 2 + 1))
        if h == 0:
            p.wait("tensor", ev_s, NT4)
            p.wait("tensor", ld, ld.n)
            p.wait("tensor", ldg, 16)
            p.wait("tensor", cst, 1)
        projs = ([("q", 0, tc) for tc in range(4)] +
                 [("k", 64, tc) for tc in range(8)] +
                 [("v", 128, tc) for tc in range(8)])
        for kind, wof, tc in projs:
            bk = banks[n_p % 2]
            if n_p >= 2:
                p.wait("tensor", ev_p, n_p - 1)
            for c in range(ND):
                def f(e, bk=bk, c=c, wof=wof, tc=tc, sl=sl):
                    return e.matmul(bk[0:64, :],
                                    lhsT=wqh[sl][:, c * 192 + wof:c * 192 + wof + 64],
                                    rhs=fm[:, c * S4 + tc * 512:c * S4 + (tc + 1) * 512],
                                    start=(c == 0), stop=(c == ND - 1))
                if c == ND - 1:
                    p.on("tensor", f, inc=(pe_p, 1))
                else:
                    p.on("tensor", f)
            n_p += 1
            p.wait("vector", pe_p, n_p)
            if kind == "q":
                dstp = qh[0:64, tc * 512:(tc + 1) * 512]
            elif kind == "k":
                dstp = kh[0:64, tc * 512:(tc + 1) * 512]
            else:
                dstp = vstage[0:64, tc * 512:(tc + 1) * 512]
            p.on("vector", lambda e, bk=bk, dstp=dstp: e.tensor_copy(
                out=dstp, in_=bk[0:64, :]), inc=(ev_p, 1))
        p.wait("tensor", ev_p, n_p)
        for vt in range(32):
            bk = banks[2 + (n_vt % 2)]
            if n_vt >= 2:
                p.wait("tensor", ev_vt, n_vt - 1)
            def f(e, bk=bk, vt=vt):
                return e.transpose(bk[:].bitcast(BF16)[0:128, 0:64],
                                   in_=vstage[0:64, vt * 128:(vt + 1) * 128],
                                   identity=ident[0:64, 0:64])
            p.on("tensor", f, inc=(pe_vt, 1))
            n_vt += 1
            p.wait("vector", pe_vt, n_vt)
            p.on("vector", lambda e, bk=bk, vt=vt: e.tensor_copy(
                out=vtm[:, vt * 64:(vt + 1) * 64], in_=bk[:].bitcast(BF16)[0:128, 0:64]),
                inc=(ev_vt, 1))
        for qi in range(16):
            lo_t = 4 * (qi // 4) - 4
            ktiles = [(t if t >= 0 else 32 + t) for t in range(lo_t, qi + 1)]
            nkt = len(ktiles)
            for mi, kt in enumerate(ktiles):
                bk = banks[4 + (n_S % 2)]
                if n_S >= 2:
                    p.wait("tensor", dve_S, n_S - 1)
                def f(e, bk=bk, kt=kt, qi=qi):
                    return e.matmul(bk[:, 0:128], lhsT=qh[0:64, qi * 128:(qi + 1) * 128],
                                    rhs=kh[0:64, kt * 128:(kt + 1) * 128],
                                    start=True, stop=True)
                p.on("tensor", f, inc=(pe_S, 1))
                n_S += 1
                p.wait("vector", pe_S, n_S)
                if mi == 0:
                    p.wait("vector", act_P, n_qt)
                crosshalf = (lo_t + mi < 0)
                isdiag = (mi == nkt - 1)

                def fev(e, bk=bk, mi=mi, crosshalf=crosshalf, isdiag=isdiag):
                    e.tensor_scalar(out=Sb[:, mi * 128:(mi + 1) * 128], in0=bk[:, 0:128],
                                    scalar1=SCL, scalar2=None, op0=ALU.mult)
                    e.drain()
                    if crosshalf:
                        e.tensor_scalar(out=Sb[:, mi * 128:(mi + 1) * 128],
                                        in0=Sb[:, mi * 128:(mi + 1) * 128],
                                        scalar1=flg[:, 0:1], scalar2=flg[:, 1:2],
                                        op0=ALU.mult, op1=ALU.add)
                        e.drain()
                    if isdiag:
                        e.tensor_tensor(out=Sb[:, mi * 128:(mi + 1) * 128],
                                        in0=Sb[:, mi * 128:(mi + 1) * 128],
                                        in1=mskd[:], op=ALU.mult)
                        e.drain()
                        e.tensor_tensor(out=Sb[:, mi * 128:(mi + 1) * 128],
                                        in0=Sb[:, mi * 128:(mi + 1) * 128],
                                        in1=mskdb[:], op=ALU.add)
                        e.drain()
                    return e.nop()
                p.on("vector", fev, inc=(dve_S, 1))
            W = nkt * 128

            def fst(e, W=W):
                e.tensor_reduce(out=sst_t[:, 0:1], in_=Sb[:, 0:W], axis=AX.X, op=ALU.max)
                e.drain()
                return e.tensor_scalar(out=sst_t[:, 1:2], in0=sst_t[:, 0:1],
                                       scalar1=-1.0, scalar2=None, op0=ALU.mult)
            p.on("vector", fst, inc=(dve_P, 1))
            p.wait("scalar", dve_P, dve_P.n)
            p.wait("scalar", pe_V, n_V)
            p.on("scalar", lambda e, W=W: e.activation(
                out=Pb[:, 0:W], in_=Sb[:, 0:W], func=AF.Exp,
                bias=sst_t[:, 1:2], accum_out=sst_t[:, 2:3]), inc=(act_P, 1))
            n_qt += 1
            p.wait("vector", act_P, n_qt)

            def fl(e):
                e.reciprocal(out=sst_t[:, 3:4], in_=sst_t[:, 2:3])
                return e.drain()
            p.on("vector", fl, inc=(dve_P, 1))
            bk = banks[2 + (n_T % 2)]
            p.wait("tensor", ev_T, ev_T.n)
            p.wait("tensor", dve_P, dve_P.n)
            p.wait("tensor", ev_V, ev_V.n)
            for c in range(nkt):
                def f(e, bk=bk, c=c, last=(c == nkt - 1)):
                    return e.transpose(bk[:].bitcast(BF16).rearrange(
                        "p (a r) -> p a r", r=128)[:, c, :],
                        in_=Pb[:, c * 128:(c + 1) * 128], identity=ident[:])
                if c == nkt - 1:
                    p.on("tensor", f, inc=(pe_T, 1))
                else:
                    p.on("tensor", f)
            n_T += 1
            p.wait("vector", pe_T, n_T)
            p.wait("vector", pe_V, n_V)
            p.on("vector", lambda e, bk=bk, W=W: e.tensor_copy(
                out=ptb[:, 0:W], in_=bk[:].bitcast(BF16)[:, 0:W]), inc=(ev_T, 1))
            p.wait("tensor", ev_T, ev_T.n)
            for c in range(nkt):
                kt = ktiles[c]
                def f(e, c=c, kt=kt, first=(c == 0), last=(c == nkt - 1)):
                    return e.matmul(banks[6][:, 0:64],
                                    lhsT=ptb[:, c * 128:(c + 1) * 128],
                                    rhs=vtm[:, kt * 64:(kt + 1) * 64],
                                    start=first, stop=last)
                if c == nkt - 1:
                    p.on("tensor", f, inc=(pe_V, 1))
                else:
                    p.on("tensor", f)
            n_V += 1
            p.wait("vector", pe_V, n_V)
            prow = (h % 2) * 64
            ycol = (h // 2) * T + qi * 128
            p.on("vector", lambda e: e.tensor_scalar(
                out=ptb[:, 0:64], in0=banks[6][:, 0:64],
                scalar1=sst_t[:, 3:4], scalar2=None, op0=ALU.mult), inc=(ev_V, 1))
            p.wait("tensor", ev_V, ev_V.n)
            def ft(e):
                return e.transpose(banks[7][:].bitcast(BF16)[0:64, 0:128],
                                   in_=ptb[:, 0:64], identity=ident[:])
            p.on("tensor", ft, inc=(pe_T, 1))
            n_T += 1
            p.wait("vector", pe_T, n_T)
            p.on("vector", lambda e, prow=prow, ycol=ycol: e.tensor_copy(
                out=yfm[prow:prow + 64, ycol:ycol + 128],
                in_=banks[7][:].bitcast(BF16)[0:64, 0:128]), inc=(ev_T, 1))
    for i in range(NT):
        p.wait("tensor", ev_T, ev_T.n)
        p.wait("tensor", dve_O, dve_O.n)
        for oh in range(2):
            for c in range(ND):
                def f(e, i=i, oh=oh, c=c, last=(c == ND - 1)):
                    return e.matmul(banks[4 + oh][:],
                                    lhsT=yfm[:, c * T + i * 128:c * T + (i + 1) * 128],
                                    rhs=wo[:, c * D + oh * 512:c * D + oh * 512 + 512],
                                    start=(c == 0), stop=last)
                if c == ND - 1:
                    p.on("tensor", f, inc=(pe_O, 1))
                else:
                    p.on("tensor", f)
        sl = (i % 2) * D
        if i >= 2:
            p.wait("sync", st[i % 2], 16 * (i // 2))
        p.dma("sync", xr[:, sl:sl + D], xv[:, i, :], ldr)
        p.wait("vector", ldr, ldr.n)
        p.wait("vector", pe_O, pe_O.n)
        for oh in range(2):
            def f(e, oh=oh, sl=sl):
                return e.tensor_tensor(out=xr[:, sl + oh * 512:sl + oh * 512 + 512],
                                       in0=banks[4 + oh][:],
                                       in1=xr[:, sl + oh * 512:sl + oh * 512 + 512],
                                       op=ALU.add)
            p.on("vector", f, inc=(dve_O, 1))
        p.wait("sync", dve_O, dve_O.n)
        p.dma("sync", y_d.rearrange("(i p) d -> p i d", p=128)[:, i, :],
              xr[:, sl:sl + D], st[i % 2])
    p.wait("sync", st[0])
    p.wait("sync", st[1])
    p.emit_and_close()
    return nc


def run_l3_attn(x_full, w_qkv, w_out):
    if "l3_attn" not in _NC_CACHE:
        _NC_CACHE["l3_attn"] = build_l3_attn()
    nc = _NC_CACHE["l3_attn"]
    w_qkv = np.asarray(w_qkv, np.float32)
    w_out = np.asarray(w_out, np.float32)
    wqp = np.ascontiguousarray(w_qkv.reshape(8, 128, 3072).transpose(1, 0, 2).reshape(128, 8 * 3072))
    wop = np.ascontiguousarray(w_out.reshape(8, 128, 1024).transpose(1, 0, 2).reshape(128, 8 * 1024))
    in_maps = []
    for c in range(NCORE):
        flag = 1.0 if c % 2 == 1 else 0.0
        flagv = np.zeros((128, 2), np.float32)
        flagv[:, 0] = flag
        flagv[:, 1] = (flag - 1.0) * 30000.0
        in_maps.append({"x": np.ascontiguousarray(_rot(x_full, c)),
                        "wqp": wqp, "wop": wop, "flag3": flagv})
    res = run_bass_kernel_spmd(nc, in_maps, core_ids=list(range(NCORE)))
    return np.concatenate([res.results[c]["y"] for c in range(NCORE)], axis=0)


def kernel(**inputs):
    g = {k: np.asarray(v) for k, v in inputs.items()}
    x = np.ascontiguousarray(np.asarray(g["L0_attn_q"], np.float32).reshape(16384, 1024))
    x = run_l0_ffn(x, g["L0_ffn_rail_maker"], g["L0_ffn_rail_up_proj"], g["L0_ffn_w_cproj"])
    x = run_l0_attn(x, g["L0_attn_k"], g["L0_attn_v"])
    x = run_l0_ffn(x, g["L0_ffn_rail_maker"], g["L0_ffn_rail_up_proj"], g["L0_ffn_w_cproj"])
    x = run_l1_ffn(x, g["L1_ffn_w1"], g["L1_ffn_conv_w"], g["L1_ffn_conv_b"], g["L1_ffn_w2"])
    x = run_l1_attn(x, g["L1_attn_W1"], g["L1_attn_b1"], g["L1_attn_Wp"], g["L1_attn_bp"])
    x = run_l1_ffn(x, g["L1_ffn_w1"], g["L1_ffn_conv_w"], g["L1_ffn_conv_b"], g["L1_ffn_w2"])
    x = run_l2_ffn(x, g["L2_ffn_gate_weight"], g["L2_ffn_gate_up_proj"], g["L2_ffn_down_proj"])
    x = run_l2_attn(x, g["L2_attn_w_attn"], g["L2_attn_w_proj"])
    x = run_l2_ffn(x, g["L2_ffn_gate_weight"], g["L2_ffn_gate_up_proj"], g["L2_ffn_down_proj"])
    wgT, wuT, wdT = run_l3_wq(g["L3_ffn_w_gate"], g["L3_ffn_w_up"], g["L3_ffn_w_down"])
    x = run_l3_ffn(x, wgT, wuT, wdT)
    x = run_l3_attn(x, g["L3_attn_w_qkv"], g["L3_attn_w_out"])
    x = run_l3_ffn(x, wgT, wuT, wdT)
    return x.reshape(4, 4096, 1024)
